# version 03
# baseline (speedup 1.0000x reference)
import numpy as np
from contextlib import ExitStack

import concourse.bass as bass
import concourse.mybir as mybir

F32 = mybir.dt.float32
BF16 = mybir.dt.bfloat16
AF = mybir.ActivationFunctionType
ALU = mybir.AluOpType
AX = mybir.AxisListType

S, SQ, E, H, HD = 4096, 2048, 1024, 16, 64
NEXP, HID = 8, 4096
EPS = 1e-5
SCL = 0.125
SEMS = ("pe", "dve", "act", "pool", "sout", "gout",
        "gs", "gb", "gw", "sx", "ks0", "ks1", "ks2", "ks3", "nk",
        "kp0", "kp1", "vp0", "vp1", "tt0", "tt1", "tcr0", "tcr1",
        "bb0", "bb1", "bb2", "bb3", "bdg0", "bdg1", "bdg2", "bdg3", "cbw", "cbr",
        "xo0", "xo1", "xw0", "xw1", "xr60", "xr61", "wo70", "wo71",
        "wi70", "wi71", "wi72", "wi73",
        "ms70", "ms71", "ms72", "ms73", "ms74", "ms75", "ms76", "ms77",
        "mr80", "mr81",
        "xr80", "xr81", "go5", "so0", "so1")
ENGS = ("sync", "gpsimd", "tensor", "vector", "scalar")


class MK:
    def __init__(self, nc, upto="full", debug=(), mini=False, cut=None):
        self.nc = nc
        self.upto = upto
        self.mini = mini
        self.cut = cut
        self.debug = set(debug)
        self.cnt = {s: 0 for s in SEMS}
        self.semh = {}
        self.cur = None
        self.last_wait = None
        self.dbg_names = []

    def _rec(self, engine, fn, wait=(), inc=None, inc_by=1):
        wl = []
        for (s, val) in wait:
            if val is None or val <= 0:
                continue
            if self.last_wait[engine].get(s, -1) >= val:
                continue
            self.last_wait[engine][s] = val
            wl.append((s, val))
        self.cur[engine].append((wl, fn, inc, inc_by))
        if inc is not None:
            self.cnt[inc] += inc_by
            return self.cnt[inc]
        return None

    def t(self, fn, wait=(), inc="pe"):
        return self._rec("tensor", fn, wait, inc)

    def v(self, fn, wait=(), inc="dve"):
        return self._rec("vector", fn, wait, inc)

    def s(self, fn, wait=(), inc="act"):
        return self._rec("scalar", fn, wait, inc)

    def g(self, fn, wait=(), inc="pool"):
        return self._rec("gpsimd", fn, wait, inc)

    def gd(self, fn, wait=(), sem="gs"):
        return self._rec("gpsimd", fn, wait, sem, 16)

    def hd(self, fn, wait=(), sem="sx"):
        return self._rec("sync", fn, wait, sem, 16)

    def so(self, fn, wait=()):
        return self._rec("sync", fn, wait, "sout", 16)

    def barrier(self):
        snap = [(s, self.cnt[s]) for s in self.cnt]
        for e in ENGS:
            self._rec(e, lambda eng: None, wait=snap)

    class _Phase:
        def __init__(self, mk, name):
            self.mk, self.name = mk, name

        def __enter__(self):
            self.mk.cur = {e: [] for e in ENGS}
            self.mk.last_wait = {e: {} for e in ENGS}
            self.mk.barrier()
            return self

        def __exit__(self, et, ev, tb):
            if et is not None:
                return False
            mk = self.mk
            with mk.nc.Block(self.name) as block:
                for e in ENGS:
                    ops = mk.cur[e]
                    if not ops:
                        continue

                    def body(eng, ops=ops):
                        for wl, fn, inc, inc_by in ops:
                            for (s, val) in wl:
                                eng.wait_ge(mk.semh[s], val)
                            ins = fn(eng)
                            if inc is not None:
                                assert ins is not None
                                ins.then_inc(mk.semh[inc], inc_by)
                    getattr(block, e)(body)
            mk.cur = None
            return False

    PH_ORDER = ["P0_consts", "P1_weights", "P2_ln1_qkv", "P3_norms",
                "P4_attn", "P5_oproj", "P6_ln2_gates", "P7_mbuild",
                "P8_apply", "P_final"]

    def enabled(self, name):
        if self.upto == "full":
            return True
        lim = {"p0": 0, "p1": 1, "p2": 2, "p3": 3, "p4": 4, "p5": 5,
               "p6": 6, "p7": 7, "p8": 8}.get(self.upto, 99)
        i = self.PH_ORDER.index(name) if name in self.PH_ORDER else 99
        return i <= lim or name == "P_final"

    class _NullPhase:
        def __enter__(self):
            return self

        def __exit__(self, et, ev, tb):
            return False

    def phase(self, name):
        if not self.enabled(name):
            return MK._Skip(self, name)
        return MK._Phase(self, name)

    class _Skip:
        def __init__(self, mk, name):
            self.mk = mk

        def __enter__(self):
            self.mk.cur = {e: [] for e in ENGS}
            self.mk.last_wait = {e: {} for e in ENGS}
            self.saved_cnt = dict(self.mk.cnt)
            return self

        def __exit__(self, et, ev, tb):
            if et is not None:
                return False
            self.mk.cnt = self.saved_cnt
            self.mk.cur = None
            return False

    def dbg(self, name, src_ap):
        if name not in self.debug or self.cur is None:
            return
        shape = list(src_ap.shape)
        h = self.nc.dram_tensor("dbg_" + name, shape, F32,
                                kind="ExternalOutput").ap()
        self.dbg_names.append("dbg_" + name)
        snap = [(s, self.cnt[s]) for s in self.cnt if s not in ("sout", "gout")]

        def f(g, h=h, src=src_ap):
            return g.dma_start(out=h, in_=src)
        self._rec("gpsimd", f, wait=snap, inc="gout", inc_by=16)


def build(upto="full", debug=(), mini=False, cut=None):
    nc = bass.Bass("TRN2", target_bir_lowering=False)
    mk = MK(nc, upto, debug, mini, cut)

    x_in = nc.dram_tensor("x_local", [S, E], F32, kind="ExternalInput").ap()
    Wd = {n: nc.dram_tensor(n, [E, E], F32, kind="ExternalInput").ap()
          for n in ("Wq", "Wk", "Wv", "Wo")}
    bd = {n: nc.dram_tensor(n, [E], F32, kind="ExternalInput").ap()
          for n in ("bq", "bk", "bv", "bo")}
    relk_d = nc.dram_tensor("relk", [41, HD], F32, kind="ExternalInput").ap()
    relv_d = nc.dram_tensor("relv", [41, HD], F32, kind="ExternalInput").ap()
    wgate_d = nc.dram_tensor("wgate", [NEXP, E], F32, kind="ExternalInput").ap()
    win_d = nc.dram_tensor("win", [NEXP, HID, E], F32, kind="ExternalInput").ap()
    wout_d = nc.dram_tensor("wout", [NEXP, E, HID], F32, kind="ExternalInput").ap()
    m_in = nc.dram_tensor("m_upper", [128, 1], F32, kind="ExternalInput").ap()
    y_out = nc.dram_tensor("y", [SQ, E], F32, kind="ExternalOutput").ap()

    k_bf = nc.dram_tensor("k_bf", [E, S], BF16).ap()
    v_bf = nc.dram_tensor("v_bf", [S, E], BF16).ap()
    TW = 295
    ttab = nc.dram_tensor("ttab", [4 * 128 * TW], F32).ap()
    dgt = nc.dram_tensor("dgt", [4 * (128 * 168 + 64)], BF16).ap()
    ctab = nc.dram_tensor("ctab", [2 * (128 * 21 + 32)], BF16).ap()
    xnew_sc = nc.dram_tensor("xnew_sc", [SQ, E], F32).ap()
    m_sc = nc.dram_tensor("m_sc", [NEXP, E, E], BF16).ap()

    outer = ExitStack()
    with outer:
        for s in SEMS:
            mk.semh[s] = outer.enter_context(nc.semaphore(s))

        def sb(st, name, p, f, dt):
            return st.enter_context(nc.sbuf_tensor(name, [p, f], dt))

        def psum(st, name, p, f, dt):
            return st.enter_context(nc.psum_tensor(name, [p, f], dt))

        ident = sb(outer, "ident", 128, 128, BF16)
        ident_f = sb(outer, "ident_f", 128, 128, F32)
        ones_bf = sb(outer, "ones_bf", 1, 128, BF16)
        ones_f32 = sb(outer, "ones_f32", 1, 128, F32)
        m_sb = sb(outer, "m_sb", 128, 1, F32)
        mn_sb = sb(outer, "mn_sb", 128, 1, F32)
        mneg_sb = sb(outer, "mneg_sb", 128, 1, F32)
        zgap = sb(outer, "zgap", 128, 32, BF16)
        eps_col = sb(outer, "eps_col", 128, 1, F32)
        mask_tl = sb(outer, "mask_tl", 128, 168, BF16)
        mask_tr = sb(outer, "mask_tr", 128, 168, BF16)
        mask_tl0 = sb(outer, "mask_tl0", 128, 148, BF16)
        mask_tr0 = sb(outer, "mask_tr0", 128, 148, BF16)
        mask_148 = sb(outer, "mask_148", 128, 41, BF16)
        mask_20 = sb(outer, "mask_20", 128, 20, BF16)
        mask_trb = sb(outer, "mask_trb", 128, 20, BF16)
        relk_nat = sb(outer, "relk_nat", 41, 64, BF16)
        relkT = sb(outer, "relkT", 128, 41, BF16)
        relv_sb = sb(outer, "relv_sb", 41, 64, BF16)
        wg_nat = sb(outer, "wg_nat", 8, E, BF16)
        wgT = sb(outer, "wgT", 128, 64, BF16)
        bq_sb = sb(outer, "bq_sb", 128, 8, F32)
        bk_sb = sb(outer, "bk_sb", 128, 8, F32)
        bo_row = sb(outer, "bo_row", 1, E, F32)
        bv_row = sb(outer, "bv_row", 1, E, BF16)
        mt_sb = sb(outer, "mt_sb", 128, 256, F32)
        gates_sb = sb(outer, "gates_sb", 128, 16 * 8, F32)
        sel16 = sb(outer, "sel16", 128, 8 * 16, BF16)



        with mk.phase("P0_consts"):
            def f(g):
                g.memset(ident[:], 1.0)
                g.memset(ident_f[:], 1.0)
                g.memset(ones_bf[:], 1.0)
                g.memset(ones_f32[:], 1.0)
                g.memset(zgap[:], 0.0)
                g.memset(eps_col[:], EPS)
                g.memset(sel16[:], 0.0)
                g.memset(mask_tl[:], 1.0)
                g.memset(mask_tr[:], 1.0)
                g.memset(mask_tl0[:], 1.0)
                g.memset(mask_tr0[:], 1.0)
                g.memset(mask_148[:], 1.0)
                g.memset(mask_20[:], 1.0)
                return g.memset(mask_trb[:], 1.0)
            c0 = mk.g(f)

            def f(g):
                g.affine_select(out=ident[:], in_=ident[:], pattern=[[-1, 128]],
                                compare_op=ALU.is_equal, fill=0.0, base=0,
                                channel_multiplier=1)
                g.affine_select(out=ident_f[:], in_=ident_f[:], pattern=[[-1, 128]],
                                compare_op=ALU.is_equal, fill=0.0, base=0,
                                channel_multiplier=1)
                g.affine_select(out=mask_tl[:], in_=mask_tl[:], pattern=[[-1, 168]],
                                compare_op=ALU.is_gt, fill=0.0, base=0,
                                channel_multiplier=1)
                g.affine_select(out=mask_tr[:], in_=mask_tr[:], pattern=[[1, 168]],
                                compare_op=ALU.is_ge, fill=0.0, base=-41,
                                channel_multiplier=-1)
                g.affine_select(out=mask_tl0[:], in_=mask_tl0[:], pattern=[[-1, 148]],
                                compare_op=ALU.is_gt, fill=0.0, base=-21,
                                channel_multiplier=1)
                g.affine_select(out=mask_tr0[:], in_=mask_tr0[:], pattern=[[1, 148]],
                                compare_op=ALU.is_ge, fill=0.0, base=-21,
                                channel_multiplier=-1)
                g.affine_select(out=mask_148[:], in_=mask_148[:], pattern=[[1, 41]],
                                compare_op=ALU.is_ge, fill=0.0, base=-148,
                                channel_multiplier=1)
                g.affine_select(out=mask_20[:], in_=mask_20[:], pattern=[[-1, 20]],
                                compare_op=ALU.is_ge, fill=0.0, base=19,
                                channel_multiplier=-1)
                return g.affine_select(out=mask_trb[:], in_=mask_trb[:],
                                       pattern=[[1, 20]], compare_op=ALU.is_ge,
                                       fill=0.0, base=107, channel_multiplier=-1)
            c1 = mk.g(f, wait=[("pool", c0)])

            def f(g):
                last = None
                for c in range(8):
                    g.memset(sel16[0:64, c * 16 + 2 * c: c * 16 + 2 * c + 1], 1.0)
                    last = g.memset(sel16[64:128, c * 16 + 2 * c + 1: c * 16 + 2 * c + 2], 1.0)
                return last
            c2 = mk.g(f, wait=[("pool", c1)])

            for srcdst in ("m", "relk", "relv", "wg", "bv"):
                def f(g, srcdst=srcdst):
                    if srcdst == "m":
                        return g.dma_start(out=m_sb[:], in_=m_in)
                    if srcdst == "relk":
                        return g.dma_start(out=relk_nat[:], in_=relk_d)
                    if srcdst == "relv":
                        return g.dma_start(out=relv_sb[:], in_=relv_d)
                    if srcdst == "wg":
                        return g.dma_start(out=wg_nat[:], in_=wgate_d)
                    return g.dma_start(out=bv_row[:], in_=bd["bv"].rearrange("(o e) -> o e", o=1))
                mk.gd(f, wait=[("pool", c2)])
            g_small = mk.cnt["gs"]

            def f(v):
                v.tensor_scalar(out=mn_sb[:], in0=m_sb[:], scalar1=-1.0,
                                scalar2=1.0, op0=ALU.mult, op1=ALU.add)
                return v.tensor_scalar(out=mneg_sb[:], in0=m_sb[:], scalar1=-1.0,
                                       scalar2=None, op0=ALU.mult)
            mk.v(f, wait=[("gs", g_small)])

            def f(sy):
                return sy.dma_start(out=bo_row[:], in_=bd["bo"].rearrange("(o e) -> o e", o=1))
            mk.hd(f, sem="gb")

        sAB = ExitStack()
        with sAB:
            w_sb = sb(sAB, "w_sb", 128, 8 * SQ, BF16)
            ps_mf = psum(sAB, "ps_mf", 128, 512, F32)

            sA = ExitStack()
            with sA:
                q_fm = sb(sA, "q_fm", 128, 8 * SQ, BF16)

                sQW = ExitStack()
                WqT = sb(sQW, "WqT", 128, 8 * E, BF16)
                WkT = sb(sQW, "WkT", 128, 8 * E, BF16)
                WvT = sb(sQW, "WvT", 128, 8 * E, BF16)
                sW = ExitStack()
                with sW:
                    wnat = sb(sW, "wnat", 128, 8 * E, BF16)
                    bq_row = sb(sW, "bq_row", 1, E, F32)
                    bk_row = sb(sW, "bk_row", 1, E, F32)
                    ps_wt = psum(sW, "ps_wt", 128, 2048, BF16)

                    with mk.phase("P1_weights"):
                        def f(sy):
                            return sy.dma_start(out=bq_row[:], in_=bd["bq"].rearrange("(o e) -> o e", o=1))
                        mk.hd(f, sem="gb")
                        def f(sy):
                            return sy.dma_start(out=bk_row[:], in_=bd["bk"].rearrange("(o e) -> o e", o=1))
                        mk.hd(f, sem="gb")
                        g_bias_rows = mk.cnt["gb"]
                        def f(t):
                            return t.transpose(ps_wt[0:64, 0:41], relk_nat[:],
                                               ident[0:41, 0:41])
                        pk = mk.t(f, wait=[("gs", g_small), ("pool", mk.cnt["pool"])])

                        def f(v):
                            v.tensor_copy(relkT[0:64, :], ps_wt[0:64, 0:41])
                            return v.tensor_copy(relkT[64:128, :], ps_wt[0:64, 0:41])
                        d_rk = mk.v(f, wait=[("pe", pk)])

                        def f(t):
                            last = None
                            for dc in range(8):
                                last = t.transpose(ps_wt[0:128, 1024 + dc * 8:1024 + dc * 8 + 8],
                                                   wg_nat[:, dc * 128:(dc + 1) * 128],
                                                   ident[0:8, 0:8])
                            return last
                        pw = mk.t(f, wait=[("dve", d_rk)])

                        def f(v):
                            return v.tensor_copy(wgT[:], ps_wt[:, 1024:1024 + 64])
                        d_wg = mk.v(f, wait=[("pe", pw)])

                        def f(t):
                            last = None
                            for oc in range(8):
                                last = t.transpose(ps_mf[:, oc:oc + 1],
                                                   bq_row[0:1, oc * 128:(oc + 1) * 128],
                                                   ident_f[0:1, 0:1])
                            for oc in range(8):
                                last = t.transpose(ps_mf[:, 8 + oc:8 + oc + 1],
                                                   bk_row[0:1, oc * 128:(oc + 1) * 128],
                                                   ident_f[0:1, 0:1])
                            return last
                        p_b = mk.t(f, wait=[("gb", g_bias_rows), ("dve", d_wg), ("pool", mk.cnt["pool"])])

                        def f(v):
                            v.tensor_copy(bq_sb[:], ps_mf[:, 0:8])
                            return v.tensor_copy(bk_sb[:], ps_mf[:, 8:16])
                        g_bias_d = mk.v(f, wait=[("pe", p_b)])

                        prev_tail = [0, 0]
                        for wname, dst in (("Wq", WqT), ("Wk", WkT), ("Wv", WvT)):
                            def f(g, wname=wname):
                                src = bass.AP(Wd[wname].tensor, 0,
                                              [[E, 128], [128 * E, 8], [1, E]])
                                return g.dma_start(out=wnat[:], in_=src)
                            gl = mk.gd(f, wait=[("pe", mk.cnt["pe"])], sem="gw")
                            ev_marks = []
                            for c in range(8):
                                def ftr(t, c=c):
                                    last = None
                                    for dc in range(8):
                                        o = (c % 2) * 1024 + dc * 128
                                        last = t.transpose(
                                            ps_wt[:, o:o + 128],
                                            wnat[:, c * E + dc * 128:c * E + dc * 128 + 128],
                                            ident[:])
                                    return last
                                waits = [("gw", gl), ("dve", g_bias_d)]
                                waits.append(("dve", ev_marks[c - 2] if c >= 2 else prev_tail[c]))
                                pc = mk.t(ftr, wait=waits)

                                def fev(v, c=c, dst=dst):
                                    dest = bass.AP(dst, c * 128,
                                                   [[8 * E, 128], [E, 8], [1, 128]])
                                    return v.tensor_copy(dest,
                                                         ps_wt[:, (c % 2) * 1024:(c % 2) * 1024 + 1024])
                                ev_marks.append(mk.v(fev, wait=[("pe", pc)]))
                            prev_tail = ev_marks[6:8]
                            mk.dbg("WT_" + wname, dst[:])

                    sW.close()
                    sX = ExitStack()
                    with sX:
                        xt = sb(sX, "xt", 128, 4 * E, F32)
                        x1c = sb(sX, "x1c", 128, 2 * 4 * E, BF16)
                        x1f = sb(sX, "x1f", 128, 2 * 8 * 512, BF16)
                        stq = sb(sX, "stq", 128, 128, F32)
                        kv_ev = sb(sX, "kv_ev", 128, 4 * 512, BF16)
                        ps_qkv = psum(sX, "ps_qkv", 128, 2 * 512, F32)
                        ps_v = psum(sX, "ps_v", 128, 2 * 512, F32)
                        ps_x1t = psum(sX, "ps_x1t", 128, 2 * 1024, BF16)

                        with mk.phase("P2_ln1_qkv"):
                            xb_free_all = 0
                            x1t_free = [0, 0]
                            qkv_bank_free = [0, 0]
                            v_bank_free = [0, 0]
                            kv_slot_free = [0, 0, 0, 0]
                            stat_free = [0] * 8
                            for tc in range(8):
                                buf = tc % 2
                                own = tc < 4

                                def f(sy, tc=tc):
                                    src = bass.AP(x_in.tensor, tc * 512 * E,
                                                  [[E, 128], [128 * E, 4], [1, E]])
                                    return sy.dma_start(out=xt[:], in_=src)
                                h_lx = mk.hd(f, wait=[("dve", xb_free_all)], sem="sx")

                                norm_marks = []
                                for tt in range(4):
                                    tix = xt[:, tt * E:(tt + 1) * E]
                                    tox = x1c[:, buf * 4 * E + tt * E: buf * 4 * E + (tt + 1) * E]
                                    g8 = buf * 4 + tt
                                    sc = g8 * 16
                                    def f(v, tix=tix, sc=sc):
                                        v.bn_stats(stq[:, sc:sc + 6], tix[:, 0:512])
                                        return v.bn_stats(stq[:, sc + 6:sc + 12],
                                                          tix[:, 512:1024])
                                    d_bn0 = mk.v(f, wait=[("sx", h_lx),
                                                          ("act", stat_free[g8])])

                                    def f(v, sc=sc):
                                        return v.bn_aggr(stq[:, sc + 12:sc + 14],
                                                         stq[:, sc:sc + 12])
                                    d_bn = mk.v(f, wait=[("dve", d_bn0)])

                                    def f(s, sc=sc):
                                        return s.activation(out=stq[:, sc + 15:sc + 16],
                                                            in_=stq[:, sc + 13:sc + 14],
                                                            func=AF.Ln, bias=eps_col[:],
                                                            scale=1.0)
                                    a1 = mk.s(f, wait=[("dve", d_bn)])

                                    def f(s, sc=sc):
                                        return s.activation(out=stq[:, sc + 14:sc + 15],
                                                            in_=stq[:, sc + 15:sc + 16],
                                                            func=AF.Exp, bias=0.0,
                                                            scale=-0.5)
                                    a2 = mk.s(f, wait=[("act", a1)])
                                    stat_free[g8] = a2

                                    def f(v, tix=tix, tox=tox, sc=sc):
                                        return v.tensor_scalar(
                                            out=tox, in0=tix,
                                            scalar1=stq[:, sc + 12:sc + 13],
                                            scalar2=stq[:, sc + 14:sc + 15],
                                            op0=ALU.subtract, op1=ALU.mult)
                                    d_norm = mk.v(f, wait=[("act", a2)])
                                    norm_marks.append(d_norm)

                                    def f(t, tox=tox, tt=tt):
                                        last = None
                                        for dc in range(8):
                                            o = (tt % 2) * 1024 + dc * 128
                                            last = t.transpose(
                                                ps_x1t[:, o:o + 128],
                                                tox[:, dc * 128:(dc + 1) * 128], ident[:])
                                        return last
                                    p_tx = mk.t(f, wait=[("dve", max(d_norm, x1t_free[tt % 2]))])

                                    def f(v, tt=tt, buf=buf):
                                        dest = bass.AP(x1f, buf * 8 * 512 + tt * 128,
                                                       [[2 * 8 * 512, 128], [512, 8], [1, 128]])
                                        return v.tensor_copy(
                                            dest, ps_x1t[:, (tt % 2) * 1024:(tt % 2) * 1024 + 1024])
                                    x1t_free[tt % 2] = mk.v(f, wait=[("pe", p_tx)])
                                xb_free_all = max(norm_marks)
                                x1_ready = mk.cnt["dve"]

                                kinds = [("k", WkT, bk_sb)] + ([("q", WqT, bq_sb)] if own else [])
                                for kind, WT, bias_cols in kinds:
                                    for oc in range(8):
                                        bank = oc % 2

                                        def fmm(t, WT=WT, oc=oc, bank=bank, buf=buf):
                                            last = None
                                            for dc in range(8):
                                                last = t.matmul(
                                                    ps_qkv[:, bank * 512:(bank + 1) * 512],
                                                    WT[:, dc * E + oc * 128: dc * E + oc * 128 + 128],
                                                    x1f[:, buf * 8 * 512 + dc * 512: buf * 8 * 512 + (dc + 1) * 512],
                                                    start=(dc == 0), stop=(dc == 7))
                                            return last
                                        pmm = mk.t(fmm, wait=[("dve", x1_ready),
                                                              ("act", qkv_bank_free[bank]),
                                                              ("dve", g_bias_d)])
                                        if kind == "q":
                                            def fev(s, oc=oc, bank=bank, tc=tc, bias_cols=bias_cols):
                                                return s.activation(
                                                    out=q_fm[:, oc * SQ + tc * 512: oc * SQ + (tc + 1) * 512],
                                                    in_=ps_qkv[:, bank * 512:(bank + 1) * 512],
                                                    func=AF.Identity,
                                                    bias=bias_cols[:, oc:oc + 1], scale=1.0)
                                            qkv_bank_free[bank] = mk.s(fev, wait=[("pe", pmm)])
                                        else:
                                            slot = oc % 4

                                            def fev(s, bank=bank, slot=slot, oc=oc, bias_cols=bias_cols):
                                                return s.activation(
                                                    out=kv_ev[:, slot * 512:(slot + 1) * 512],
                                                    in_=ps_qkv[:, bank * 512:(bank + 1) * 512],
                                                    func=AF.Identity,
                                                    bias=bias_cols[:, oc:oc + 1], scale=1.0)
                                            aev = mk.s(fev, wait=[("pe", pmm),
                                                                  ("ks%d" % slot, kv_slot_free[slot])])
                                            qkv_bank_free[bank] = aev

                                            def fdm(sy, oc=oc, slot=slot, tc=tc):
                                                return sy.dma_start(
                                                    out=k_bf[oc * 128:(oc + 1) * 128,
                                                             tc * 512:(tc + 1) * 512],
                                                    in_=kv_ev[:, slot * 512:(slot + 1) * 512])
                                            kv_slot_free[slot] = mk.hd(fdm, wait=[("act", aev)], sem="ks%d" % slot)

                                for tt in range(4):
                                    for oh in range(2):
                                        bank = (tt * 2 + oh) % 2

                                        def fmv(t, tt=tt, oh=oh, bank=bank, buf=buf):
                                            for dc in range(8):
                                                t.matmul(
                                                    ps_v[:, bank * 512:(bank + 1) * 512],
                                                    x1f[:, buf * 8 * 512 + dc * 512 + tt * 128: buf * 8 * 512 + dc * 512 + (tt + 1) * 128],
                                                    WvT[:, dc * E + oh * 512: dc * E + (oh + 1) * 512],
                                                    start=(dc == 0), stop=False)
                                            return t.matmul(
                                                ps_v[:, bank * 512:(bank + 1) * 512],
                                                ones_bf[0:1, 0:128],
                                                bv_row[0:1, oh * 512:(oh + 1) * 512],
                                                start=False, stop=True)
                                        pmv = mk.t(fmv, wait=[("dve", max(x1_ready, v_bank_free[bank])),
                                                              ("gs", g_small)])
                                        slot = (tt * 2 + oh) % 4

                                        def fev(v, bank=bank, slot=slot):
                                            return v.tensor_copy(
                                                kv_ev[:, slot * 512:(slot + 1) * 512],
                                                ps_v[:, bank * 512:(bank + 1) * 512])
                                        dev = mk.v(fev, wait=[("pe", pmv),
                                                              ("ks%d" % slot, kv_slot_free[slot])])
                                        v_bank_free[bank] = dev

                                        def fdm(sy, tc=tc, tt=tt, oh=oh, slot=slot):
                                            return sy.dma_start(
                                                out=v_bf[tc * 512 + tt * 128: tc * 512 + (tt + 1) * 128,
                                                         oh * 512:(oh + 1) * 512],
                                                in_=kv_ev[:, slot * 512:(slot + 1) * 512])
                                        kv_slot_free[slot] = mk.hd(fdm, wait=[("dve", dev)], sem="ks%d" % slot)

                            mk.dbg("q_fm", q_fm[:])
                            mk.dbg("x1f", x1f[:])

                sQW.close()
                sN = ExitStack()
                with sN:
                    nq2_sb = sb(sN, "nq2_sb", 128, SQ, F32)
                    nk2_sb = sb(sN, "nk2_sb", 128, S, F32)
                    sqk = sb(sN, "sqk", 128, S, BF16)
                    kchunk = sb(sN, "kchunk", 128, S, BF16)
                    kbv = sb(sN, "kbv", 128, 16, F32)
                    kb_row = sb(sN, "kb_row", 1, 16, F32)
                    kb_sb = sb(sN, "kb_sb", 128, 16, F32)
                    mtq_raw = sb(sN, "mtq_raw", 128, 256, F32)
                    ps_n = psum(sN, "ps_n", 128, 2 * 512, F32)

                    with mk.phase("P3_norms"):
                        nk_add = 0
                        bankfree = [0, 0]
                        for oc in range(8):
                            def f(sy, oc=oc):
                                return sy.dma_start(out=kchunk[:],
                                                    in_=k_bf[oc * 128:(oc + 1) * 128, :])
                            h_k = mk.hd(f, wait=[("pool", mk.cnt["pool"])], sem="nk")

                            def f(g, oc=oc):
                                return g.tensor_tensor(out=sqk[:], in0=kchunk[:],
                                                       in1=kchunk[:], op=ALU.mult)
                            g_sq = mk.g(f, wait=[("nk", h_k), ("pe", mk.cnt["pe"])])
                            for tcol in range(8):
                                bank = tcol % 2

                                def f(t, oc=oc, tcol=tcol, bank=bank):
                                    return t.matmul(ps_n[0:16, bank * 512:(bank + 1) * 512],
                                                    sel16[:, oc * 16:(oc + 1) * 16],
                                                    sqk[:, tcol * 512:(tcol + 1) * 512],
                                                    start=True, stop=True)
                                pm = mk.t(f, wait=[("pool", g_sq), ("dve", bankfree[bank])])

                                if oc == 0:
                                    def f(v, tcol=tcol, bank=bank):
                                        return v.tensor_copy(
                                            nk2_sb[0:16, tcol * 512:(tcol + 1) * 512],
                                            ps_n[0:16, bank * 512:(bank + 1) * 512])
                                else:
                                    def f(v, tcol=tcol, bank=bank):
                                        return v.tensor_tensor(
                                            out=nk2_sb[0:16, tcol * 512:(tcol + 1) * 512],
                                            in0=nk2_sb[0:16, tcol * 512:(tcol + 1) * 512],
                                            in1=ps_n[0:16, bank * 512:(bank + 1) * 512],
                                            op=ALU.add)
                                bankfree[bank] = mk.v(f, wait=[("pe", pm)])
                        for oc in range(8):
                            def f(g, oc=oc):
                                return g.tensor_tensor(out=sqk[:, 0:SQ],
                                                       in0=q_fm[:, oc * SQ:(oc + 1) * SQ],
                                                       in1=q_fm[:, oc * SQ:(oc + 1) * SQ],
                                                       op=ALU.mult)
                            g_sq = mk.g(f, wait=[("dve", bankfree[0]), ("dve", bankfree[1]),
                                                 ("pe", mk.cnt["pe"])])
                            for tcol in range(4):
                                bank = tcol % 2

                                def f(t, oc=oc, tcol=tcol, bank=bank):
                                    return t.matmul(ps_n[0:16, bank * 512:(bank + 1) * 512],
                                                    sel16[:, oc * 16:(oc + 1) * 16],
                                                    sqk[:, tcol * 512:(tcol + 1) * 512],
                                                    start=True, stop=True)
                                pm = mk.t(f, wait=[("pool", g_sq), ("dve", bankfree[bank])])
                                if oc == 0:
                                    def f(v, tcol=tcol, bank=bank):
                                        return v.tensor_copy(
                                            nq2_sb[0:16, tcol * 512:(tcol + 1) * 512],
                                            ps_n[0:16, bank * 512:(bank + 1) * 512])
                                else:
                                    def f(v, tcol=tcol, bank=bank):
                                        return v.tensor_tensor(
                                            out=nq2_sb[0:16, tcol * 512:(tcol + 1) * 512],
                                            in0=nq2_sb[0:16, tcol * 512:(tcol + 1) * 512],
                                            in1=ps_n[0:16, bank * 512:(bank + 1) * 512],
                                            op=ALU.add)
                                bankfree[bank] = mk.v(f, wait=[("pe", pm)])

                        def f(v):
                            return v.tensor_reduce(out=kbv[0:16, 0:1], in_=nk2_sb[0:16, :],
                                                   axis=AX.X, op=ALU.max)
                        d_km = mk.v(f, wait=[("dve", bankfree[0]), ("dve", bankfree[1])])

                        def f(s):
                            return s.activation(out=kbv[0:16, 1:2], in_=kbv[0:16, 0:1],
                                                func=AF.Ln, bias=0.0, scale=1.0)
                        a1 = mk.s(f, wait=[("dve", d_km)])

                        def f(s):
                            return s.activation(out=kbv[0:16, 2:3], in_=kbv[0:16, 1:2],
                                                func=AF.Exp, bias=0.0, scale=0.5)
                        a2 = mk.s(f, wait=[("act", a1)])

                        def f(v):
                            return v.tensor_scalar(out=kbv[0:16, 3:4], in0=kbv[0:16, 2:3],
                                                   scalar1=0.125, scalar2=0.25,
                                                   op0=ALU.mult, op1=ALU.add)
                        d_kb = mk.v(f, wait=[("act", a2)])

                        def f(t):
                            return t.transpose(ps_mf[0:1, 0:16], kbv[0:16, 3:4],
                                               ident_f[0:16, 0:16])
                        p_tk = mk.t(f, wait=[("dve", d_kb)])

                        def f(v):
                            return v.tensor_copy(kb_row[:], ps_mf[0:1, 0:16])
                        d_kr = mk.v(f, wait=[("pe", p_tk)])

                        def f(t):
                            return t.matmul(ps_mf[:, 16:32], ones_f32[0:1, :],
                                            kb_row[0:1, :], start=True, stop=True)
                        p_kb = mk.t(f, wait=[("dve", d_kr)])

                        def f(v):
                            return v.tensor_copy(kb_sb[:], ps_mf[:, 16:32])
                        d_kbb = mk.v(f, wait=[("pe", p_kb)])

                        tr_marks = []
                        for qt in range(16):
                            def f(t, qt=qt):
                                return t.transpose(ps_mf[:, 32 + (qt % 8) * 16: 32 + (qt % 8) * 16 + 16],
                                                   nq2_sb[0:16, qt * 128:(qt + 1) * 128],
                                                   ident_f[0:16, 0:16])
                            w = [("dve", mk.cnt["dve"])]
                            p_tq = mk.t(f, wait=w)

                            def f(v, qt=qt):
                                return v.tensor_copy(mtq_raw[:, qt * 16:(qt + 1) * 16],
                                                     ps_mf[:, 32 + (qt % 8) * 16: 32 + (qt % 8) * 16 + 16])
                            tr_marks.append(mk.v(f, wait=[("pe", p_tq)]))
                        d_tr = mk.cnt["dve"]

                        def f(v):
                            return v.tensor_scalar(out=mtq_raw[:], in0=mtq_raw[:],
                                                   scalar1=1e-12, scalar2=None,
                                                   op0=ALU.add)
                        d_eps = mk.v(f, wait=[("dve", d_tr)])

                        def f(s):
                            return s.activation(out=mt_sb[:], in_=mtq_raw[:],
                                                func=AF.Ln, bias=0.0, scale=1.0)
                        a3 = mk.s(f, wait=[("dve", d_eps)])

                        def f(s):
                            return s.activation(out=mtq_raw[:], in_=mt_sb[:],
                                                func=AF.Exp, bias=0.0, scale=0.5)
                        a4 = mk.s(f, wait=[("act", a3)])

                        def f(v):
                            last = None
                            for qt in range(16):
                                last = v.tensor_tensor(
                                    out=mt_sb[:, qt * 16:(qt + 1) * 16],
                                    in0=mtq_raw[:, qt * 16:(qt + 1) * 16],
                                    in1=kb_sb[:], op=ALU.mult)
                            return last
                        d_mt = mk.v(f, wait=[("act", a4), ("dve", d_kbb)])

                        mk.dbg("mt", mt_sb[:])
                        mk.dbg("nk2", nk2_sb[0:16, :])
                        mk.dbg("nq2", nq2_sb[0:16, :])
                        mk.dbg("k_bfd", k_bf)
                        mk.dbg("v_bfd", v_bf)


                sT = ExitStack()
                with sT:
                    kp_sb = sb(sT, "kp_sb", 128, 2 * S, BF16)
                    vp_sb = sb(sT, "vp_sb", 128, 2 * S, BF16)
                    exp_sb = sb(sT, "exp_sb", 128, 4 * S, BF16)
                    expT = sb(sT, "expT", 128, 32 * 512, BF16)
                    C_sb = sb(sT, "C_sb", 128, 4 * 168, F32)
                    T_sb = sb(sT, "T_sb", 128, 2 * 295, F32)
                    P_sb = sb(sT, "P_sb", 128, 4 * 48, F32)
                    dg_sb = sb(sT, "dg_sb", 128, 4 * 48, BF16)
                    cd_sb = sb(sT, "cd_sb", 128, 24, BF16)
                    tri_sb = sb(sT, "tri_sb", 128, 168, BF16)
                    aev_sb = sb(sT, "aev_sb", 128, 4 * 48, BF16)
                    aet_sb = sb(sT, "aet_sb", 41, 512, BF16)
                    sc_sb = sb(sT, "sc_sb", 128, 4 * 16, F32)
                    zg_sb = sb(sT, "zg_sb", 128, 4, F32)
                    zrow_sb = sb(sT, "zrow_sb", 1, 512, F32)
                    zrec_sb = sb(sT, "zrec_sb", 1, 512, F32)
                    rzb_sb = sb(sT, "rzb_sb", 128, 512, F32)
                    blf_sb = sb(sT, "blf_sb", 128, 16, F32)
                    ps_s = psum(sT, "ps_s", 128, 2048, F32)
                    ps_t = psum(sT, "ps_t", 128, 2048, BF16)
                    ps_w = psum(sT, "ps_w", 128, 512, F32)

                    if mk.mini is True:
                        HPS, HHS, GS = [0], [0, 1], [0, 3]
                    elif mk.mini:
                        HPS, HHS, GS = mk.mini
                    else:
                        HPS, HHS, GS = list(range(8)), [0, 1], list(range(4))

                    with mk.phase("P4_attn"):
                        def f_winit(g_):
                            return g_.memset(w_sb[:], 0.0)
                        w_init = mk.g(f_winit)

                        def f_ctz(sy):
                            return sy.dma_start(
                                out=bass.AP(ctab.tensor, 0, [[1, 128 * 22]]),
                                in_=bass.AP(zgap, 0, [[32, 128], [1, 22]]))
                        mk.hd(f_ctz, sem="cbw")
                        kp_free = [0, 0]
                        vp_free = [0, 0]
                        expfree = [dict(pe=0, pool=0, bb=0, bdg=0, act=0, cbw=0)
                                   for _ in range(4)]
                        tband_free = [0, 0]
                        tsb_free = [0, 0]
                        c_free = [0] * 4
                        dg_free = [0, 0, 0, 0]
                        zc_free = [0] * 4
                        blf_free = [0] * 4
                        psb_free = 0
                        misc_free = 0
                        pst_free = [0, 0]
                        expT_free = 0
                        psw_free = 0
                        aet_free = 0
                        exp_c1_done = 0
                        pending_tr = None
                        tb = 0
                        zg_read_pe = 0
                        aet_read_pe = 0

                        def band_geom(qt):
                            bs = max(0, 128 * qt - 20)
                            be = 128 * qt + 148
                            return bs, min(be, S)

                        def emit_tr_batches(qtl, par, nbatches, start_b):
                            nonlocal tb
                            marks = expfree[par]
                            for b in range(start_b, start_b + nbatches):
                                bank = tb % 2
                                tb += 1

                                def ftr(t, qtl=qtl, par=par, b=b, bank=bank):
                                    last = None
                                    for j in range(8):
                                        jt = 8 * b + j
                                        last = t.transpose(
                                            ps_t[:, bank * 1024 + j * 128: bank * 1024 + (j + 1) * 128],
                                            exp_sb[:, par * S + jt * 128: par * S + (jt + 1) * 128],
                                            ident[:])
                                    return last
                                pc = mk.t(ftr, wait=[("act", exp_c1_done),
                                                     ("dve", pst_free[bank]),
                                                     ("pe", expT_free)])

                                def fev(v, qtl=qtl, b=b, bank=bank):
                                    dest = bass.AP(expT, qtl * 128 + (8 * b) * 512,
                                                   [[32 * 512, 128], [512, 8], [1, 128]])
                                    return v.tensor_copy(dest, ps_t[:, bank * 1024: bank * 1024 + 1024])
                                pst_free[bank] = mk.v(fev, wait=[("pe", pc)])
                                marks["pe"] = pc

                        for hp in HPS:
                            buf = hp % 2

                            def fkl(sy, hp=hp, buf=buf):
                                return sy.dma_start(out=kp_sb[:, buf * S:(buf + 1) * S],
                                                    in_=k_bf[hp * 128:(hp + 1) * 128, :])
                            kl = mk.hd(fkl, wait=[("pe", kp_free[buf])], sem="kp%d" % buf)

                            def fvl(sy, hp=hp, buf=buf):
                                src = bass.AP(v_bf.tensor, hp * 128,
                                              [[E, 128], [128 * E, 32], [1, 128]])
                                return sy.dma_start(out=vp_sb[:, buf * S:(buf + 1) * S], in_=src)
                            vl = mk.hd(fvl, wait=[("pe", vp_free[buf])], sem="vp%d" % buf)

                            for hh in HHS:
                                h = 2 * hp + hh
                                qb = 64 * hh
                                for g in GS:
                                    if mk.cut is not None and hh == 1 and mk.cut <= 1:
                                        continue
                                    ae_done_marks = []
                                    group_marks = []
                                    def fpm(t, hp=hp, hh=hh, g=g, qb=qb):
                                        last = None
                                        for qtl in range(4):
                                            qt = 4 * g + qtl
                                            qsl = q_fm[qb:qb + 64,
                                                       hp * SQ + qt * 128: hp * SQ + (qt + 1) * 128]
                                            last = t.matmul(ps_mf[:, qtl * 48: qtl * 48 + 41],
                                                            qsl, relkT[qb:qb + 64, :],
                                                            start=True, stop=True)
                                        return last
                                    ppm = mk.t(fpm, wait=[("dve", misc_free),
                                                          ("gs", mk.cnt["gs"])])

                                    def fpe(v):
                                        last = None
                                        for qtl in range(4):
                                            last = v.tensor_copy(
                                                P_sb[:, qtl * 48: qtl * 48 + 41],
                                                ps_mf[:, qtl * 48: qtl * 48 + 41])
                                        return last
                                    d_pe = mk.v(fpe, wait=[("pe", ppm), ("pool", psb_free)])
                                    misc_free = d_pe

                                    for qtl in range(4):
                                        qt = 4 * g + qtl
                                        par = qtl % 2
                                        po = qtl * 48
                                        mtcol = mt_sb[:, qt * 16 + h: qt * 16 + h + 1]

                                        def fb1(g_, po=po, qtl=qtl, mtcol=mtcol):
                                            return g_.tensor_scalar(
                                                out=blf_sb[:, qtl * 4: qtl * 4 + 1],
                                                in0=P_sb[:, po:po + 1], scalar1=SCL,
                                                scalar2=mtcol, op0=ALU.mult, op1=ALU.subtract)
                                        b1 = mk.v(fb1, wait=[("dve", d_pe), ("act", blf_free[qtl])])

                                        def fb2(g_, po=po, qtl=qtl, mtcol=mtcol):
                                            return g_.tensor_scalar(
                                                out=blf_sb[:, qtl * 4 + 2: qtl * 4 + 3],
                                                in0=P_sb[:, po + 40:po + 41], scalar1=SCL,
                                                scalar2=mtcol, op0=ALU.mult, op1=ALU.subtract)
                                        b2 = mk.v(fb2, wait=[("dve", b1)])

                                        def fb3(g_, qtl=qtl):
                                            return g_.tensor_scalar(
                                                out=blf_sb[:, qtl * 4 + 3: qtl * 4 + 4],
                                                in0=blf_sb[:, qtl * 4: qtl * 4 + 1],
                                                scalar1=m_sb[:, 0:1], scalar2=None,
                                                op0=ALU.mult)
                                        b3 = mk.v(fb3, wait=[("dve", b2)])

                                        def fb4(g_, qtl=qtl):
                                            return g_.scalar_tensor_tensor(
                                                out=blf_sb[:, qtl * 4 + 1: qtl * 4 + 2],
                                                in0=blf_sb[:, qtl * 4 + 2: qtl * 4 + 3],
                                                scalar=mn_sb[:, 0:1],
                                                in1=blf_sb[:, qtl * 4 + 3: qtl * 4 + 4],
                                                op0=ALU.mult, op1=ALU.add)
                                        b4 = mk.v(fb4, wait=[("dve", b3)])
                                        blf_ready = b4

                                        def ft1(g_, par=par):
                                            return g_.memset(T_sb[:, par * 295: par * 295 + 128], 0.0)
                                        t1 = mk.g(ft1, wait=[("tt%d" % par, tsb_free[par])])

                                        def ft2(g_, par=par, po=po):
                                            return g_.tensor_scalar(
                                                out=T_sb[:, par * 295 + 127: par * 295 + 168],
                                                in0=P_sb[:, po:po + 41],
                                                scalar1=P_sb[:, po:po + 1], scalar2=None,
                                                op0=ALU.subtract)
                                        t2 = mk.v(ft2, wait=[("pool", t1), ("dve", b4)])

                                        def ft3(g_, par=par, po=po):
                                            return g_.tensor_scalar(
                                                out=T_sb[:, par * 295 + 168: par * 295 + 295],
                                                in0=T_sb[:, par * 295: par * 295 + 127],
                                                scalar1=P_sb[:, po + 40:po + 41],
                                                scalar2=P_sb[:, po:po + 1],
                                                op0=ALU.add, op1=ALU.subtract)
                                        t3 = mk.v(ft3, wait=[("dve", t2)])

                                        def ftw(sy, par=par):
                                            dst = bass.AP(ttab.tensor, par * 128 * 295,
                                                          [[295, 128], [1, 295]])
                                            return sy.dma_start(out=dst, in_=T_sb[:, par * 295:(par + 1) * 295])
                                        tw = mk.hd(ftw, wait=[("dve", t3),
                                                              ("tcr%d" % par, tband_free[par])],
                                                   sem="tt%d" % par)
                                        tsb_free[par] = tw

                                        def ftc(sy, par=par, qtl=qtl):
                                            srcap = bass.AP(ttab.tensor, par * 128 * 295 + 127,
                                                            [[294, 128], [1, 168]])
                                            return sy.dma_start(out=C_sb[:, qtl * 168:(qtl + 1) * 168],
                                                                in_=srcap)
                                        tc = mk.hd(ftc, wait=[("tt%d" % par, tw),
                                                              ("dve", c_free[qtl])],
                                                   sem="tcr%d" % par)
                                        tband_free[par] = tc
                                        c_ready = tc

                                        group_marks.append((blf_ready, c_ready))
                                    psb_free = mk.cnt["pool"]

                                    if mk.cut is not None and hh == 1 and mk.cut <= 2:
                                        continue
                                    for qtl in range(4):
                                        qt = 4 * g + qtl
                                        par = qtl
                                        bs, be = band_geom(qt)
                                        blf_ready, c_ready = group_marks[qtl]
                                        zbase = qtl * 16
                                        def fs0(t, hp=hp, qb=qb, qt=qt, buf=buf):
                                            qsl = q_fm[qb:qb + 64,
                                                       hp * SQ + qt * 128: hp * SQ + (qt + 1) * 128]
                                            last = None
                                            for n in range(4):
                                                last = t.matmul(
                                                    ps_s[:, n * 512:(n + 1) * 512], qsl,
                                                    kp_sb[qb:qb + 64, buf * S + n * 512: buf * S + (n + 1) * 512],
                                                    start=True, stop=True)
                                            return last
                                        ps0 = mk.t(fs0, wait=[("act", exp_c1_done),
                                                              ("kp%d" % buf, kl)])
                                        kp_free[buf] = ps0
                                        if pending_tr is not None:
                                            emit_tr_batches(*pending_tr, 2, 0)
                                        stt_marks = []
                                        if qt <= 14:
                                            def fba(v, qtl=qtl, bs=bs, be=be, qt=qt):
                                                c0 = 20 if qt == 0 else 0
                                                return v.tensor_tensor(
                                                    out=ps_s[:, bs:be],
                                                    in0=ps_s[:, bs:be],
                                                    in1=C_sb[:, qtl * 168 + c0: qtl * 168 + c0 + (be - bs)],
                                                    op=ALU.add)
                                            sm = mk.v(fba, wait=[("pe", ps0), ("tcr%d" % (qtl % 2), c_ready)])
                                            stt_marks.append(sm)
                                        else:
                                            def fba(v, qtl=qtl):
                                                return v.tensor_tensor(
                                                    out=ps_s[:, 1900:2048],
                                                    in0=ps_s[:, 1900:2048],
                                                    in1=C_sb[:, qtl * 168: qtl * 168 + 148],
                                                    op=ALU.add)
                                            sm = mk.v(fba, wait=[("pe", ps0), ("tcr%d" % (qtl % 2), c_ready)])
                                            stt_marks.append(sm)
                                        zsl = []
                                        if qt <= 14:
                                            r0 = [(0, be, 0), (be, 2048, 2)]
                                        else:
                                            r0 = [(0, 2048, 0)]
                                        acts0 = []
                                        for (a0, a1, bk) in r0:
                                            zslot = len(zsl)
                                            zsl.append(bk)

                                            def fex(s, a0=a0, a1=a1, bk=bk, qtl=qtl, par=par, zslot=zslot, zbase=zbase):
                                                bias = blf_sb[:, qtl * 4 + bk: qtl * 4 + bk + 1]
                                                return s.activation(
                                                    out=exp_sb[:, par * S + a0: par * S + a1],
                                                    in_=ps_s[:, a0:a1], func=AF.Exp,
                                                    bias=bias, scale=SCL,
                                                    accum_out=sc_sb[:, zbase + zslot: zbase + zslot + 1])
                                            am = mk.s(fex, wait=[("dve", max(stt_marks[-1], blf_ready)),
                                                                 ("pe", expfree[par]["pe"]),
                                                                 ("pool", max(expfree[par]["pool"], zc_free[qtl][0] if isinstance(zc_free[qtl], tuple) else zc_free[qtl])),
                                                                 ("dve", zc_free[qtl][1] if isinstance(zc_free[qtl], tuple) else 0),
                                                                 ("bb%d" % par, expfree[par]["bb"]),
                                                                 ("bdg%d" % par, expfree[par]["bdg"]),
                                                                 ("cbw", expfree[par]["cbw"])])
                                            acts0.append(am)
                                        exp_c0_done = acts0[-1]
                                        def fs1(t, hp=hp, qb=qb, qt=qt, buf=buf):
                                            qsl = q_fm[qb:qb + 64,
                                                       hp * SQ + qt * 128: hp * SQ + (qt + 1) * 128]
                                            last = None
                                            for n in range(4):
                                                last = t.matmul(
                                                    ps_s[:, n * 512:(n + 1) * 512], qsl,
                                                    kp_sb[qb:qb + 64, buf * S + 2048 + n * 512: buf * S + 2048 + (n + 1) * 512],
                                                    start=True, stop=True)
                                            return last
                                        ps1 = mk.t(fs1, wait=[("act", exp_c0_done)])
                                        kp_free[buf] = ps1
                                        if pending_tr is not None:
                                            emit_tr_batches(*pending_tr, 2, 2)
                                            pending_tr = None
                                        stt1 = []
                                        if qt == 0:
                                            def fc2(v, qtl=qtl):
                                                return v.scalar_tensor_tensor(
                                                    out=ps_s[:, 2028:2048],
                                                    in0=C_sb[:, qtl * 168: qtl * 168 + 20],
                                                    scalar=m_sb[:, 0:1],
                                                    in1=ps_s[:, 2028:2048],
                                                    op0=ALU.mult, op1=ALU.add)
                                            stt1.append(mk.v(fc2, wait=[("pe", ps1)]))
                                        if qt == 15:
                                            def fc3(v, qtl=qtl):
                                                return v.scalar_tensor_tensor(
                                                    out=ps_s[:, 0:20],
                                                    in0=C_sb[:, qtl * 168 + 148: qtl * 168 + 168],
                                                    scalar=mn_sb[:, 0:1],
                                                    in1=ps_s[:, 0:20],
                                                    op0=ALU.mult, op1=ALU.add)
                                            stt1.append(mk.v(fc3, wait=[("pe", ps1)]))
                                        c_free[qtl] = stt_marks[-1] if not stt1 else stt1[-1]
                                        if qt == 15:
                                            r1 = [(0, 20, 0), (20, 2048, 1)]
                                        else:
                                            r1 = [(0, 2048, 1)]
                                        acts1 = []
                                        for (a0, a1, bk) in r1:
                                            zslot = len(zsl)
                                            zsl.append(bk)

                                            def fex(s, a0=a0, a1=a1, bk=bk, qtl=qtl, par=par, zslot=zslot, zbase=zbase):
                                                bias = blf_sb[:, qtl * 4 + bk: qtl * 4 + bk + 1]
                                                return s.activation(
                                                    out=exp_sb[:, par * S + 2048 + a0: par * S + 2048 + a1],
                                                    in_=ps_s[:, a0:a1], func=AF.Exp,
                                                    bias=bias, scale=SCL,
                                                    accum_out=sc_sb[:, zbase + zslot: zbase + zslot + 1])
                                            am = mk.s(fex, wait=[("dve", stt1[-1] if stt1 else stt_marks[-1]),
                                                                 ("pe", ps1)])
                                            acts1.append(am)
                                        exp_c1_done = acts1[-1]
                                        expfree[par]["act"] = exp_c1_done
                                        band_src_done = exp_c0_done if qt <= 14 else exp_c1_done

                                        def fbw(sy, par=par, qt=qt, bs=bs, be=be):
                                            base = par * (128 * 168 + 64)
                                            if qt == 0:
                                                return sy.dma_start(
                                                    out=bass.AP(dgt.tensor, base + 20, [[168, 128], [1, 148]]),
                                                    in_=exp_sb[:, par * S: par * S + 148])
                                            return sy.dma_start(
                                                out=bass.AP(dgt.tensor, base, [[168, 128], [1, 168]]),
                                                in_=exp_sb[:, par * S + bs: par * S + be])
                                        bw = mk.hd(fbw, wait=[("act", band_src_done),
                                                              ("bdg%d" % par, expfree[par]["bdg"])],
                                                   sem="bb%d" % par)
                                        expfree[par]["bb"] = bw
                                        if qt == 0:
                                            def fbz(sy, par=par):
                                                base = par * (128 * 168 + 64)
                                                return sy.dma_start(
                                                    out=bass.AP(dgt.tensor, base, [[168, 128], [1, 20]]),
                                                    in_=zgap[:, 0:20])
                                            bw = mk.hd(fbz, wait=[], sem="bb%d" % par)
                                            expfree[par]["bb"] = bw

                                        def fdg(sy, par=par):
                                            base = par * (128 * 168 + 64)
                                            return sy.dma_start(
                                                out=dg_sb[:, par * 48: par * 48 + 41],
                                                in_=bass.AP(dgt.tensor, base, [[169, 128], [1, 41]]))
                                        dgr = mk.hd(fdg, wait=[("bb%d" % par, bw),
                                                               ("pool", dg_free[par][0] if isinstance(dg_free[par], tuple) else dg_free[par]),
                                                               ("dve", dg_free[par][1] if isinstance(dg_free[par], tuple) else 0)],
                                                    sem="bdg%d" % par)
                                        expfree[par]["bdg"] = dgr
                                        if qt == 0:
                                            def fcw(sy, par=par):
                                                return sy.dma_start(
                                                    out=bass.AP(ctab.tensor, 0, [[21, 128], [1, 20]]),
                                                    in_=exp_sb[:, par * S + 4076: par * S + 4096])
                                            cw = mk.hd(fcw, wait=[("act", exp_c1_done),
                                                                  ("cbr", mk.cnt["cbr"]),
                                                                  ("cbw", mk.cnt["cbw"])],
                                                       sem="cbw")
                                            expfree[par]["cbw"] = cw

                                            def fcr(sy):
                                                return sy.dma_start(
                                                    out=cd_sb[:, 0:20],
                                                    in_=bass.AP(ctab.tensor, 0, [[22, 128], [1, 20]]))
                                            cr = mk.hd(fcr, wait=[("cbw", cw), ("pool", mk.cnt["pool"])],
                                                       sem="cbr")
                                        chain = {"pool": mk.cnt["pool"], "dve": 0}

                                        def pstep(fn, extra=()):
                                            m = mk.g(fn, wait=[("pool", chain["pool"]),
                                                               ("dve", chain["dve"])] + list(extra))
                                            chain["pool"] = m
                                            return m

                                        def vstep(fn, extra=()):
                                            m = mk.v(fn, wait=[("pool", chain["pool"]),
                                                               ("dve", chain["dve"])] + list(extra))
                                            chain["dve"] = m
                                            return m

                                        if qt == 0:
                                            tl_mask, tr_mask, tw_ = mask_tl0, mask_tr0, 148
                                            tslice = exp_sb[:, par * S: par * S + 148]
                                        else:
                                            tl_mask, tr_mask, tw_ = mask_tl, mask_tr, 168
                                            tslice = exp_sb[:, par * S + bs: par * S + be]

                                        pstep(lambda g_, tslice=tslice, tl_mask=tl_mask, tw_=tw_:
                                              g_.tensor_tensor(out=tri_sb[:, 0:tw_], in0=tslice,
                                                               in1=tl_mask[:], op=ALU.mult),
                                              extra=[("act", band_src_done)])
                                        vstep(lambda v, zbase=zbase, tw_=tw_:
                                              v.tensor_reduce(out=sc_sb[:, zbase + 4: zbase + 5],
                                                              in_=tri_sb[:, 0:tw_], axis=AX.X,
                                                              op=ALU.add))
                                        pstep(lambda g_, tslice=tslice, tr_mask=tr_mask, tw_=tw_:
                                              g_.tensor_tensor(out=tri_sb[:, 0:tw_], in0=tslice,
                                                               in1=tr_mask[:], op=ALU.mult))
                                        vstep(lambda v, zbase=zbase, tw_=tw_:
                                              v.tensor_reduce(out=sc_sb[:, zbase + 5: zbase + 6],
                                                              in_=tri_sb[:, 0:tw_], axis=AX.X,
                                                              op=ALU.add))
                                        if qt == 15:
                                            pstep(lambda g_, par=par, bs=bs:
                                                  g_.tensor_tensor(out=tri_sb[:, 0:20],
                                                                   in0=exp_sb[:, par * S + bs + 148: par * S + bs + 168],
                                                                   in1=mask_trb[:], op=ALU.mult),
                                                  extra=[("act", exp_c1_done)])
                                            vstep(lambda v, zbase=zbase:
                                                  v.tensor_reduce(out=sc_sb[:, zbase + 6: zbase + 7],
                                                                  in_=tri_sb[:, 0:20], axis=AX.X,
                                                                  op=ALU.add))
                                        vstep(lambda v, par=par, zbase=zbase:
                                              v.tensor_reduce(out=sc_sb[:, zbase + 7: zbase + 8],
                                                              in_=dg_sb[:, par * 48: par * 48 + 41],
                                                              axis=AX.X, op=ALU.add),
                                              extra=[("bdg%d" % par, dgr)])
                                        Lsl = [i for i, b in enumerate(zsl) if b == 0]
                                        Fsl = [i for i, b in enumerate(zsl) if b == 1]
                                        Rsl = [i for i, b in enumerate(zsl) if b == 2]
                                        zmk = max(acts0[-1], acts1[-1])
                                        po = qtl * 48
                                        for extra_i in Lsl[1:]:
                                            pstep(lambda g_, zbase=zbase, b0=Lsl[0], e0=extra_i:
                                                  g_.tensor_tensor(
                                                      out=sc_sb[:, zbase + b0: zbase + b0 + 1],
                                                      in0=sc_sb[:, zbase + b0: zbase + b0 + 1],
                                                      in1=sc_sb[:, zbase + e0: zbase + e0 + 1],
                                                      op=ALU.add),
                                                  extra=[("act", zmk)])
                                        zlb, zf = Lsl[0], Fsl[0]
                                        zr = Rsl[0] if Rsl else None
                                        pstep(lambda g_, zbase=zbase, qtl=qtl, zlb=zlb, zf=zf:
                                              g_.tensor_tensor(out=zg_sb[:, qtl:qtl + 1],
                                                               in0=sc_sb[:, zbase + zlb: zbase + zlb + 1],
                                                               in1=sc_sb[:, zbase + zf: zbase + zf + 1],
                                                               op=ALU.add),
                                              extra=[("act", zmk), ("pe", zg_read_pe)])
                                        if zr is not None:
                                            pstep(lambda g_, zbase=zbase, qtl=qtl, zr=zr:
                                                  g_.tensor_tensor(out=zg_sb[:, qtl:qtl + 1],
                                                                   in0=zg_sb[:, qtl:qtl + 1],
                                                                   in1=sc_sb[:, zbase + zr: zbase + zr + 1],
                                                                   op=ALU.add))
                                        if qt == 15:
                                            pstep(lambda g_, par=par:
                                                  g_.tensor_tensor(out=tri_sb[:, 0:41],
                                                                   in0=dg_sb[:, par * 48: par * 48 + 41],
                                                                   in1=mask_148[:], op=ALU.mult))
                                            vstep(lambda v, par=par, po=po:
                                                  v.scalar_tensor_tensor(
                                                      out=aev_sb[:, po:po + 41],
                                                      in0=tri_sb[:, 0:41],
                                                      scalar=mneg_sb[:, 0:1],
                                                      in1=dg_sb[:, par * 48: par * 48 + 41],
                                                      op0=ALU.mult, op1=ALU.add),
                                                  extra=[("pe", aet_read_pe)])
                                            vstep(lambda v, zbase=zbase:
                                                  v.tensor_reduce(out=sc_sb[:, zbase + 8: zbase + 9],
                                                                  in_=tri_sb[:, 0:41], axis=AX.X,
                                                                  op=ALU.add))
                                        else:
                                            pstep(lambda g_, par=par, po=po:
                                                  g_.tensor_copy(out=aev_sb[:, po:po + 41],
                                                                 in_=dg_sb[:, par * 48: par * 48 + 41]),
                                                  extra=[("pe", aet_read_pe)])
                                        pstep(lambda g_, zbase=zbase, zf=zf:
                                              g_.tensor_copy(out=sc_sb[:, zbase + 3: zbase + 4],
                                                             in_=sc_sb[:, zbase + zf: zbase + zf + 1]))
                                        if qt == 15:
                                            for addsl in (8, 6):
                                                pstep(lambda g_, zbase=zbase, addsl=addsl:
                                                      g_.tensor_tensor(
                                                          out=sc_sb[:, zbase + 3: zbase + 4],
                                                          in0=sc_sb[:, zbase + 3: zbase + 4],
                                                          in1=sc_sb[:, zbase + addsl: zbase + addsl + 1],
                                                          op=ALU.add))
                                        if qt == 0:
                                            pstep(lambda g_:
                                                  g_.tensor_tensor(out=cd_sb[:, 0:20],
                                                                   in0=cd_sb[:, 0:20],
                                                                   in1=mask_20[:], op=ALU.mult),
                                                  extra=[("cbr", cr)])
                                            vstep(lambda v, zbase=zbase:
                                                  v.tensor_reduce(out=sc_sb[:, zbase + 6: zbase + 7],
                                                                  in_=cd_sb[:, 0:20], axis=AX.X,
                                                                  op=ALU.add))
                                            pstep(lambda g_, zbase=zbase:
                                                  g_.tensor_tensor(
                                                      out=sc_sb[:, zbase + 3: zbase + 4],
                                                      in0=sc_sb[:, zbase + 3: zbase + 4],
                                                      in1=sc_sb[:, zbase + 6: zbase + 7],
                                                      op=ALU.subtract))
                                        pstep(lambda g_, zbase=zbase, zlb=zlb:
                                              g_.tensor_tensor(out=sc_sb[:, zbase + zlb: zbase + zlb + 1],
                                                               in0=sc_sb[:, zbase + zlb: zbase + zlb + 1],
                                                               in1=sc_sb[:, zbase + 5: zbase + 6],
                                                               op=ALU.subtract))
                                        pstep(lambda g_, zbase=zbase, zlb=zlb:
                                              g_.tensor_tensor(out=sc_sb[:, zbase + zlb: zbase + zlb + 1],
                                                               in0=sc_sb[:, zbase + zlb: zbase + zlb + 1],
                                                               in1=sc_sb[:, zbase + 7: zbase + 8],
                                                               op=ALU.subtract))
                                        pstep(lambda g_, zbase=zbase, zlb=zlb, par=par:
                                              g_.tensor_tensor(out=sc_sb[:, zbase + zlb: zbase + zlb + 1],
                                                               in0=sc_sb[:, zbase + zlb: zbase + zlb + 1],
                                                               in1=dg_sb[:, par * 48: par * 48 + 1],
                                                               op=ALU.add))
                                        vstep(lambda v, zbase=zbase, zlb=zlb, po=po:
                                              v.scalar_tensor_tensor(
                                                  out=aev_sb[:, po:po + 1],
                                                  in0=sc_sb[:, zbase + 3: zbase + 4],
                                                  scalar=m_sb[:, 0:1],
                                                  in1=sc_sb[:, zbase + zlb: zbase + zlb + 1],
                                                  op0=ALU.mult, op1=ALU.add))
                                        vstep(lambda v, zbase=zbase, zf=zf, po=po:
                                              v.scalar_tensor_tensor(
                                                  out=aev_sb[:, po + 40:po + 41],
                                                  in0=sc_sb[:, zbase + zf: zbase + zf + 1],
                                                  scalar=mn_sb[:, 0:1],
                                                  in1=sc_sb[:, zbase + 5: zbase + 6],
                                                  op0=ALU.mult, op1=ALU.add))
                                        if zr is not None:
                                            pstep(lambda g_, zbase=zbase, zr=zr, po=po:
                                                  g_.tensor_tensor(
                                                      out=aev_sb[:, po + 40:po + 41],
                                                      in0=aev_sb[:, po + 40:po + 41],
                                                      in1=sc_sb[:, zbase + zr: zbase + zr + 1],
                                                      op=ALU.add))
                                        if qt == 15:
                                            vstep(lambda v, zbase=zbase, po=po:
                                                  v.scalar_tensor_tensor(
                                                      out=aev_sb[:, po + 40:po + 41],
                                                      in0=sc_sb[:, zbase + 6: zbase + 7],
                                                      scalar=mneg_sb[:, 0:1],
                                                      in1=aev_sb[:, po + 40:po + 41],
                                                      op0=ALU.mult, op1=ALU.add))
                                        if qt == 0:
                                            vstep(lambda v, po=po:
                                                  v.scalar_tensor_tensor(
                                                      out=aev_sb[:, po:po + 20],
                                                      in0=cd_sb[:, 0:20],
                                                      scalar=m_sb[:, 0:1],
                                                      in1=aev_sb[:, po:po + 20],
                                                      op0=ALU.mult, op1=ALU.add))
                                        ae_tail = chain["pool"]
                                        ae_dve_tail = chain["dve"]
                                        dg_free[par] = (ae_tail, ae_dve_tail)
                                        zc_free[qtl] = (ae_tail, ae_dve_tail)
                                        blf_free[qtl] = exp_c1_done
                                        expfree[par]["pool"] = ae_tail
                                        ae_done_marks.append((ae_tail, ae_dve_tail))
                                        pending_tr = (qtl, par)

                                    if mk.cut is not None and hh == 1 and mk.cut <= 3:
                                        pending_tr = None
                                        continue
                                    emit_tr_batches(*pending_tr, 4, 0)
                                    pending_tr = None
                                    def fav1(t, hh=hh, buf=buf):
                                        last = None
                                        for jt in range(32):
                                            last = t.matmul(
                                                ps_w[0:64, 0:512],
                                                vp_sb[:, buf * S + jt * 128 + 64 * hh: buf * S + jt * 128 + 64 * hh + 64],
                                                expT[:, jt * 512:(jt + 1) * 512],
                                                start=(jt == 0), stop=False)
                                        return last
                                    pav1 = mk.t(fav1, wait=[("dve", max(pst_free[0], pst_free[1])),
                                                            ("vp%d" % buf, vl),
                                                            ("dve", psw_free)])
                                    vp_free[buf] = pav1
                                    bank = tb % 2
                                    tb += 1

                                    def fat(t, bank=bank):
                                        last = None
                                        for qtl in range(4):
                                            last = t.transpose(
                                                ps_t[0:41, bank * 1024 + qtl * 128: bank * 1024 + (qtl + 1) * 128],
                                                aev_sb[:, qtl * 48: qtl * 48 + 41],
                                                ident[:])
                                        return last
                                    pat = mk.t(fat, wait=[("pool", max(x[0] for x in ae_done_marks)),
                                                          ("dve", max(max(x[1] for x in ae_done_marks), pst_free[bank]))])

                                    def faev(v, bank=bank):
                                        return v.tensor_copy(aet_sb[:],
                                                             ps_t[0:41, bank * 1024: bank * 1024 + 512])
                                    d_aet = mk.v(faev, wait=[("pe", pat)])
                                    pst_free[bank] = d_aet
                                    def fzt_(t):
                                        last = None
                                        for qtl in range(4):
                                            last = t.transpose(ps_mf[0:1, qtl * 128:(qtl + 1) * 128],
                                                               zg_sb[:, qtl:qtl + 1], ident_f[:])
                                        return last
                                    pzt = mk.t(fzt_, wait=[("pool", max(x[0] for x in ae_done_marks)),
                                                           ("dve", max(max(x[1] for x in ae_done_marks), misc_free))])
                                    zg_read_pe = pzt

                                    def fze(v):
                                        return v.tensor_copy(zrow_sb[:], ps_mf[0:1, 0:512])
                                    d_ze = mk.v(fze, wait=[("pe", pzt)])

                                    def fzr(v):
                                        return v.reciprocal(zrec_sb[:], zrow_sb[:])
                                    d_zr = mk.v(fzr, wait=[("dve", d_ze)])

                                    def fzb(t):
                                        return t.matmul(ps_mf[0:64, 0:512], ones_f32[0:1, 0:64],
                                                        zrec_sb[0:1, :], start=True, stop=True)
                                    pzb = mk.t(fzb, wait=[("dve", d_zr)])

                                    def fre(v):
                                        return v.tensor_copy(rzb_sb[0:64, :], ps_mf[0:64, 0:512])
                                    d_re = mk.v(fre, wait=[("pe", pzb)])
                                    misc_free = d_re
                                    def fav2(t):
                                        last = None
                                        for qtl in range(4):
                                            last = t.matmul(
                                                ps_w[0:64, qtl * 128:(qtl + 1) * 128],
                                                relv_sb[0:41, 0:64],
                                                aet_sb[0:41, qtl * 128:(qtl + 1) * 128],
                                                start=False, stop=(qtl == 3))
                                        return last
                                    pav = mk.t(fav2, wait=[("dve", d_aet),
                                                           ("gs", mk.cnt["gs"])])
                                    expT_free = pav1
                                    aet_read_pe = pav

                                    def fwe(v, hh=hh, hp=hp, g=g):
                                        return v.tensor_tensor(
                                            out=w_sb[64 * hh:64 * hh + 64,
                                                     hp * SQ + g * 512: hp * SQ + (g + 1) * 512],
                                            in0=ps_w[0:64, :], in1=rzb_sb[0:64, :],
                                            op=ALU.mult)
                                    psw_free = mk.v(fwe, wait=[("pe", pav), ("dve", d_re),
                                                               ("pool", w_init)])

                        mk.dbg("w_sb", w_sb[:])
                        mk.dbg("aev0", aev_sb[:, 0:41])
                        mk.dbg("aev3", aev_sb[:, 144:185])
                        mk.dbg("zg", zg_sb[:])
                        mk.dbg("P0", P_sb[:, 0:41])
                        mk.dbg("sc", sc_sb[:, 0:8])
                        mk.dbg("rzb", rzb_sb[0:64, :])
                        mk.dbg("expd", exp_sb[:])
                        mk.dbg("blf", blf_sb[:])


            s5 = ExitStack()
            with s5:
                WoT = sb(s5, "WoT", 128, 8 * E, BF16)
                wnat5 = sb(s5, "wnat5", 128, 8 * E, BF16)
                ao_sb = sb(s5, "ao_sb", 128, 2 * E, F32)
                xo_sb = sb(s5, "xo_sb", 128, 2 * E, F32)
                on_sb = sb(s5, "on_sb", 128, 2 * E, F32)
                bo_bc = sb(s5, "bo_bc", 128, E, F32)
                st5 = sb(s5, "st5", 128, 64, F32)
                ps_o = psum(s5, "ps_o", 128, 1024, F32)
                ps_t5 = psum(s5, "ps_t5", 128, 2048, BF16)

                with mk.phase("P5_oproj"):
                    def f(g):
                        srcw = bass.AP(Wd["Wo"].tensor, 0,
                                       [[E, 128], [128 * E, 8], [1, E]])
                        return g.dma_start(out=wnat5[:], in_=srcw)
                    gl5 = mk.gd(f, sem="gw")
                    ev5 = []
                    for c in range(8):
                        def ftr(t, c=c):
                            last = None
                            for dc in range(8):
                                o = (c % 2) * 1024 + dc * 128
                                last = t.transpose(
                                    ps_t5[:, o:o + 128],
                                    wnat5[:, c * E + dc * 128:c * E + dc * 128 + 128],
                                    ident[:])
                            return last
                        waits = [("gw", gl5)]
                        if c >= 2:
                            waits.append(("dve", ev5[c - 2]))
                        pc = mk.t(ftr, wait=waits)

                        def fev(v, c=c):
                            dest = bass.AP(WoT, c * 128,
                                           [[8 * E, 128], [E, 8], [1, 128]])
                            return v.tensor_copy(dest,
                                                 ps_t5[:, (c % 2) * 1024:(c % 2) * 1024 + 1024])
                        ev5.append(mk.v(fev, wait=[("pe", pc)]))
                    wot_ready = ev5[-1]

                    def fbo(t):
                        t.matmul(ps_o[:, 0:512], ones_f32[0:1, :],
                                 bo_row[0:1, 0:512], start=True, stop=True)
                        return t.matmul(ps_o[:, 512:1024], ones_f32[0:1, :],
                                        bo_row[0:1, 512:1024], start=True, stop=True)
                    p_bo = mk.t(fbo, wait=[("gb", mk.cnt["gb"])])

                    def febo(v):
                        return v.tensor_copy(bo_bc[:], ps_o[:, 0:1024])
                    d_bo = mk.v(febo, wait=[("pe", p_bo)])

                    obank_free = [d_bo, d_bo]
                    xo_free = [0, 0]
                    on_free = [0, 0]
                    st_free = [0] * 2
                    for tt in range(16):
                        buf = tt % 2
                        def fx(sy, tt=tt, buf=buf):
                            srcx = bass.AP(x_in.tensor, tt * 128 * E,
                                           [[E, 128], [1, E]])
                            return sy.dma_start(out=xo_sb[:, buf * E:(buf + 1) * E],
                                                in_=srcx)
                        h_x = mk.hd(fx, wait=[("dve", xo_free[buf])], sem="xo%d" % buf)

                        for oc in range(2):
                            def fmo(t, tt=tt, oc=oc):
                                last = None
                                for dc in range(8):
                                    last = t.matmul(
                                        ps_o[:, oc * 512:(oc + 1) * 512],
                                        w_sb[:, dc * SQ + tt * 128: dc * SQ + (tt + 1) * 128],
                                        WoT[:, dc * E + oc * 512: dc * E + (oc + 1) * 512],
                                        start=(dc == 0), stop=(dc == 7))
                                return last
                            pmo = mk.t(fmo, wait=[("dve", max(wot_ready, obank_free[oc]))])

                            def feo(v, oc=oc, buf=buf):
                                return v.tensor_tensor(
                                    out=ao_sb[:, buf * E + oc * 512: buf * E + (oc + 1) * 512],
                                    in0=ps_o[:, oc * 512:(oc + 1) * 512],
                                    in1=bo_bc[:, oc * 512:(oc + 1) * 512], op=ALU.add)
                            obank_free[oc] = mk.v(feo, wait=[("pe", pmo)])
                        ao_done = obank_free[1]
                        sc5 = buf * 32
                        def fbn(v, buf=buf, sc5=sc5):
                            v.bn_stats(st5[:, sc5:sc5 + 6], ao_sb[:, buf * E: buf * E + 512])
                            return v.bn_stats(st5[:, sc5 + 6:sc5 + 12],
                                              ao_sb[:, buf * E + 512: buf * E + 1024])
                        d_b0 = mk.v(fbn, wait=[("dve", ao_done), ("act", st_free[buf])])

                        def fag(v, sc5=sc5):
                            return v.bn_aggr(st5[:, sc5 + 12:sc5 + 14], st5[:, sc5:sc5 + 12])
                        d_ag = mk.v(fag, wait=[("dve", d_b0)])

                        def fln(s, sc5=sc5):
                            return s.activation(out=st5[:, sc5 + 15:sc5 + 16],
                                                in_=st5[:, sc5 + 13:sc5 + 14],
                                                func=AF.Ln, bias=eps_col[:], scale=1.0)
                        a_l = mk.s(fln, wait=[("dve", d_ag)])

                        def fex5(s, sc5=sc5):
                            return s.activation(out=st5[:, sc5 + 14:sc5 + 15],
                                                in_=st5[:, sc5 + 15:sc5 + 16],
                                                func=AF.Exp, bias=0.0, scale=-0.5)
                        a_e = mk.s(fex5, wait=[("act", a_l)])
                        st_free[buf] = a_e

                        def fnm(v, buf=buf, sc5=sc5):
                            return v.tensor_scalar(
                                out=on_sb[:, buf * E:(buf + 1) * E],
                                in0=ao_sb[:, buf * E:(buf + 1) * E],
                                scalar1=st5[:, sc5 + 12:sc5 + 13],
                                scalar2=st5[:, sc5 + 14:sc5 + 15],
                                op0=ALU.subtract, op1=ALU.mult)
                        d_nm = mk.v(fnm, wait=[("act", a_e), ("xw%d" % buf, on_free[buf])])

                        def fad(v, buf=buf):
                            return v.tensor_tensor(
                                out=on_sb[:, buf * E:(buf + 1) * E],
                                in0=on_sb[:, buf * E:(buf + 1) * E],
                                in1=xo_sb[:, buf * E:(buf + 1) * E], op=ALU.add)
                        d_ad = mk.v(fad, wait=[("dve", d_nm), ("xo%d" % buf, h_x)])
                        xo_free[buf] = d_ad

                        def fwx(sy, tt=tt, buf=buf):
                            return sy.dma_start(out=xnew_sc[tt * 128:(tt + 1) * 128, :],
                                                in_=on_sb[:, buf * E:(buf + 1) * E])
                        on_free[buf] = mk.hd(fwx, wait=[("dve", d_ad)], sem="xw%d" % buf)
                    mk.dbg("xnew", xnew_sc)

        s_late = ExitStack()
        with s_late:
            x2_fm = sb(s_late, "x2_fm", 128, 8 * SQ, BF16)

            s6 = ExitStack()
            with s6:
                x6 = sb(s6, "x6", 128, 2 * E, F32)
                x2c = sb(s6, "x2c", 128, 2 * E, BF16)
                st6 = sb(s6, "st6", 128, 64, F32)
                gt6 = sb(s6, "gt6", 128, 64, F32)
                ps_l = psum(s6, "ps_l", 128, 512, F32)
                ps_t6 = psum(s6, "ps_t6", 128, 2048, BF16)

                with mk.phase("P6_ln2_gates"):
                    x6_free = [0, 0]
                    x2c_free = [0, 0]
                    st_free6 = [0, 0]
                    t6_free = [0, 0]
                    lg_free = 0
                    gt_free = [0, 0]
                    for tt in range(16):
                        buf = tt % 2

                        def fx(sy, tt=tt, buf=buf):
                            return sy.dma_start(out=x6[:, buf * E:(buf + 1) * E],
                                                in_=xnew_sc[tt * 128:(tt + 1) * 128, :])
                        h_x = mk.hd(fx, wait=[("dve", x6_free[buf]),
                                              ("xw0", mk.cnt["xw0"]), ("xw1", mk.cnt["xw1"])],
                                    sem="xr6%d" % buf)

                        sc6 = buf * 32
                        def fbn(v, buf=buf, sc6=sc6):
                            v.bn_stats(st6[:, sc6:sc6 + 6], x6[:, buf * E: buf * E + 512])
                            return v.bn_stats(st6[:, sc6 + 6:sc6 + 12],
                                              x6[:, buf * E + 512: buf * E + 1024])
                        d_b0 = mk.v(fbn, wait=[("xr6%d" % buf, h_x), ("act", st_free6[buf])])

                        def fag(v, sc6=sc6):
                            return v.bn_aggr(st6[:, sc6 + 12:sc6 + 14], st6[:, sc6:sc6 + 12])
                        d_ag = mk.v(fag, wait=[("dve", d_b0)])

                        def fln(s, sc6=sc6):
                            return s.activation(out=st6[:, sc6 + 15:sc6 + 16],
                                                in_=st6[:, sc6 + 13:sc6 + 14],
                                                func=AF.Ln, bias=eps_col[:], scale=1.0)
                        a_l = mk.s(fln, wait=[("dve", d_ag)])

                        def fex6(s, sc6=sc6):
                            return s.activation(out=st6[:, sc6 + 14:sc6 + 15],
                                                in_=st6[:, sc6 + 15:sc6 + 16],
                                                func=AF.Exp, bias=0.0, scale=-0.5)
                        a_e = mk.s(fex6, wait=[("act", a_l)])
                        st_free6[buf] = a_e

                        def fnm(v, buf=buf, sc6=sc6):
                            return v.tensor_scalar(
                                out=x2c[:, buf * E:(buf + 1) * E],
                                in0=x6[:, buf * E:(buf + 1) * E],
                                scalar1=st6[:, sc6 + 12:sc6 + 13],
                                scalar2=st6[:, sc6 + 14:sc6 + 15],
                                op0=ALU.subtract, op1=ALU.mult)
                        d_nm = mk.v(fnm, wait=[("act", a_e), ("pe", x2c_free[buf])])
                        x6_free[buf] = d_nm

                        def ftx(t, tt=tt, buf=buf):
                            last = None
                            for dc in range(8):
                                o = (tt % 2) * 1024 + dc * 128
                                last = t.transpose(
                                    ps_t6[:, o:o + 128],
                                    x2c[:, buf * E + dc * 128: buf * E + (dc + 1) * 128],
                                    ident[:])
                            return last
                        p_tx = mk.t(ftx, wait=[("dve", max(d_nm, t6_free[tt % 2]))])
                        x2c_free[buf] = p_tx

                        def fevx(v, tt=tt):
                            dest = bass.AP(x2_fm, tt * 128,
                                           [[8 * SQ, 128], [SQ, 8], [1, 128]])
                            return v.tensor_copy(dest,
                                                 ps_t6[:, (tt % 2) * 1024:(tt % 2) * 1024 + 1024])
                        d_evx = mk.v(fevx, wait=[("pe", p_tx)])
                        t6_free[tt % 2] = d_evx

                        def fml(t, tt=tt):
                            last = None
                            for dc in range(8):
                                last = t.matmul(
                                    ps_l[:, 0:8],
                                    x2_fm[:, dc * SQ + tt * 128: dc * SQ + (tt + 1) * 128],
                                    wgT[:, dc * 8:(dc + 1) * 8],
                                    start=(dc == 0), stop=(dc == 7))
                            return last
                        p_lg = mk.t(fml, wait=[("dve", max(d_evx, lg_free))])

                        gb = buf * 32
                        def fel(v, gb=gb):
                            return v.tensor_copy(gt6[:, gb:gb + 8], ps_l[:, 0:8])
                        d_lg = mk.v(fel, wait=[("pe", p_lg), ("act", gt_free[buf])])
                        lg_free = d_lg

                        gch = {"dve": d_lg, "act": 0}

                        def gv(fn, extra=()):
                            m_ = mk.v(fn, wait=[("dve", gch["dve"]), ("act", gch["act"])] + list(extra))
                            gch["dve"] = m_
                            return m_

                        def ga(fn, extra=()):
                            m_ = mk.s(fn, wait=[("dve", gch["dve"]), ("act", gch["act"])] + list(extra))
                            gch["act"] = m_
                            return m_

                        gv(lambda v, gb=gb: v.tensor_reduce(
                            out=gt6[:, gb + 8:gb + 9], in_=gt6[:, gb:gb + 8],
                            axis=AX.X, op=ALU.max))
                        gv(lambda v, gb=gb: v.tensor_scalar(
                            out=gt6[:, gb + 16:gb + 24], in0=gt6[:, gb:gb + 8],
                            scalar1=gt6[:, gb + 8:gb + 9], scalar2=None,
                            op0=ALU.is_ge))
                        gv(lambda v, gb=gb: v.scalar_tensor_tensor(
                            out=gt6[:, gb + 24:gb + 32], in0=gt6[:, gb + 16:gb + 24],
                            scalar=-1e30, in1=gt6[:, gb:gb + 8],
                            op0=ALU.mult, op1=ALU.add))
                        gv(lambda v, gb=gb: v.tensor_reduce(
                            out=gt6[:, gb + 9:gb + 10], in_=gt6[:, gb + 24:gb + 32],
                            axis=AX.X, op=ALU.max))
                        gv(lambda v, gb=gb: v.tensor_tensor(
                            out=gt6[:, gb + 10:gb + 11], in0=gt6[:, gb + 9:gb + 10],
                            in1=gt6[:, gb + 8:gb + 9], op=ALU.subtract))
                        ga(lambda s, gb=gb: s.activation(
                            out=gt6[:, gb + 11:gb + 12], in_=gt6[:, gb + 10:gb + 11],
                            func=AF.Exp, bias=0.0, scale=1.0))
                        gv(lambda v, gb=gb: v.tensor_scalar(
                            out=gt6[:, gb + 11:gb + 12], in0=gt6[:, gb + 11:gb + 12],
                            scalar1=1.0, scalar2=None, op0=ALU.add))
                        gv(lambda v, gb=gb: v.reciprocal(
                            gt6[:, gb + 12:gb + 13], gt6[:, gb + 11:gb + 12]))
                        gv(lambda v, gb=gb: v.tensor_scalar(
                            out=gt6[:, gb + 13:gb + 14], in0=gt6[:, gb + 12:gb + 13],
                            scalar1=-1.0, scalar2=1.0, op0=ALU.mult, op1=ALU.add))
                        gv(lambda v, gb=gb: v.tensor_scalar(
                            out=gt6[:, gb + 24:gb + 32], in0=gt6[:, gb + 24:gb + 32],
                            scalar1=gt6[:, gb + 9:gb + 10], scalar2=None,
                            op0=ALU.is_ge))
                        gv(lambda v, gb=gb: v.tensor_scalar(
                            out=gt6[:, gb + 16:gb + 24], in0=gt6[:, gb + 16:gb + 24],
                            scalar1=gt6[:, gb + 12:gb + 13], scalar2=None,
                            op0=ALU.mult))
                        glast = gv(lambda v, gb=gb, tt=tt: v.scalar_tensor_tensor(
                            out=gates_sb[:, tt * 8:(tt + 1) * 8],
                            in0=gt6[:, gb + 24:gb + 32],
                            scalar=gt6[:, gb + 13:gb + 14],
                            in1=gt6[:, gb + 16:gb + 24],
                            op0=ALU.mult, op1=ALU.add))
                        gt_free[buf] = gch["act"]
                    mk.dbg("gates", gates_sb[:])
                    mk.dbg("x2fm", x2_fm[:])

            s7 = ExitStack()
            with s7:
                woT = sb(s7, "woT", 128, 32 * E, BF16)
                woch = sb(s7, "woch", 128, 2 * HID, BF16)
                wsl = sb(s7, "wsl", 128, 2 * 4 * 512, BF16)
                m_ev = sb(s7, "m_ev", 128, 8 * 512, BF16)
                ps_m = psum(s7, "ps_m", 128, 4096, F32)
                psm_b = ps_m[:].bitcast(BF16)

                with mk.phase("P7_mbuild"):
                    woch_free = [0, 0]
                    wsl_free = [0, 0, 0, 0]
                    woT_free = 0
                    tr_free2 = [0, 0]
                    bank_free = [0] * 8
                    mev_free = [0] * 8
                    tr_free = 0
                    for e in range(NEXP):
                        tevs = []
                        for oc in range(8):
                            wb = oc % 2

                            def fwl(g, e=e, oc=oc, wb=wb):
                                return g.dma_start(
                                    out=woch[:, wb * HID:(wb + 1) * HID],
                                    in_=wout_d[e, oc * 128:(oc + 1) * 128, :])
                            gwo = mk.gd(fwl, wait=[("pe", woch_free[wb])], sem="wo7%d" % wb)

                            for jh in range(2):
                                def ftw(t, oc=oc, wb=wb, jh=jh):
                                    last = None
                                    for j2 in range(16):
                                        jc = jh * 16 + j2
                                        o = (jh % 2) * 2048 + j2 * 128
                                        last = t.transpose(
                                            psm_b[:, o:o + 128],
                                            woch[:, wb * HID + jc * 128: wb * HID + (jc + 1) * 128],
                                            ident[:])
                                    return last
                                waits = [("wo7%d" % wb, gwo), ("dve", tr_free2[jh % 2])]
                                if e > 0 or oc > 0:
                                    waits.append(("dve", max(bank_free)))
                                p_tw = mk.t(ftw, wait=waits)
                                if jh == 1:
                                    woch_free[wb] = p_tw

                                def fte(v, oc=oc, jh=jh):
                                    dest = bass.AP(woT, jh * 16 * E + oc * 128,
                                                   [[32 * E, 128], [E, 16], [1, 128]])
                                    return v.tensor_copy(
                                        dest, psm_b[:, (jh % 2) * 2048:(jh % 2) * 2048 + 2048])
                                tr_free2[jh % 2] = mk.v(fte, wait=[("pe", p_tw)])
                                tevs.append(tr_free2[jh % 2])
                        woT_ready = tevs[-1]

                        for dch in range(2):
                            for jq in range(8):
                                slot = jq % 2

                                def fwi(g, e=e, jq=jq, dch=dch, slot=slot):
                                    srcw = bass.AP(win_d.tensor,
                                                   e * HID * E + jq * 4 * 128 * E + dch * 512,
                                                   [[E, 128], [128 * E, 4], [1, 512]])
                                    return g.dma_start(
                                        out=wsl[:, slot * 2048:(slot + 1) * 2048], in_=srcw)
                                gwi = mk.gd(fwi, wait=[("pe", wsl_free[slot])],
                                            sem="wi7%d" % slot)

                                def fmm(t, jq=jq, dch=dch, slot=slot):
                                    last = None
                                    for j2 in range(4):
                                        jc = jq * 4 + j2
                                        for dq in range(4):
                                            for oh in range(2):
                                                bank = dq * 2 + oh
                                                last = t.matmul(
                                                    ps_m[:, bank * 512:(bank + 1) * 512],
                                                    wsl[:, slot * 2048 + j2 * 512 + dq * 128: slot * 2048 + j2 * 512 + (dq + 1) * 128],
                                                    woT[:, jc * E + oh * 512: jc * E + (oh + 1) * 512],
                                                    start=(jc == 0), stop=(jc == 31))
                                    return last
                                waits = [("wi7%d" % slot, gwi), ("dve", woT_ready)]
                                if jq == 0:
                                    waits.append(("dve", max(bank_free)))
                                    for bb_ in range(8):
                                        waits.append(("ms7%d" % bb_, mev_free[bb_]))
                                p_mm = mk.t(fmm, wait=waits)
                                wsl_free[slot] = p_mm
                            for dq in range(4):
                                for oh in range(2):
                                    bank = dq * 2 + oh

                                    def fme(v, bank=bank):
                                        return v.tensor_copy(
                                            m_ev[:, bank * 512:(bank + 1) * 512],
                                            ps_m[:, bank * 512:(bank + 1) * 512])
                                    d_me = mk.v(fme, wait=[("pe", p_mm), ("ms7%d" % bank, mev_free[bank])])
                                    bank_free[bank] = d_me

                                    def fmd(sy, e=e, dch=dch, dq=dq, oh=oh, bank=bank):
                                        dc = dch * 4 + dq
                                        return sy.dma_start(
                                            out=m_sc[e, dc * 128:(dc + 1) * 128,
                                                     oh * 512:(oh + 1) * 512],
                                            in_=m_ev[:, bank * 512:(bank + 1) * 512])
                                    mev_free[bank] = mk.hd(fmd, wait=[("dve", d_me)], sem="ms7%d" % bank)

            s8 = ExitStack()
            with s8:
                y_sb = sb(s8, "y_sb", 128, 16 * E, F32)
                M_sb = sb(s8, "M_sb", 128, 2 * 8 * E, BF16)
                x8 = sb(s8, "x8", 128, 2 * E, F32)
                o8 = sb(s8, "o8", 128, 2 * E, F32)
                st8 = sb(s8, "st8", 128, 64, F32)
                ps_y = psum(s8, "ps_y", 128, 4096, F32)

                with mk.phase("P8_apply"):
                    def fyz(g):
                        return g.memset(y_sb[:], 0.0)
                    y_init = mk.g(fyz)

                    ybank_free = [0] * 8
                    M_free = [0, 0]
                    y_dve = 0
                    for e in range(NEXP):
                        mb = e % 2

                        def fml(g, e=e, mb=mb):
                            srcm = bass.AP(m_sc.tensor, e * E * E,
                                           [[E, 128], [128 * E, 8], [1, E]])
                            return g.dma_start(out=M_sb[:, mb * 8 * E:(mb + 1) * 8 * E],
                                               in_=srcm)
                        gml = mk.gd(fml, wait=[("pe", M_free[mb])] +
                                    [("ms7%d" % bb_, mk.cnt["ms7%d" % bb_]) for bb_ in range(8)],
                                    sem="mr8%d" % mb)

                        for tt in range(16):
                            for oc in range(2):
                                bank = (tt * 2 + oc) % 8

                                def fmm(t, tt=tt, oc=oc, mb=mb, bank=bank):
                                    last = None
                                    for dc in range(8):
                                        last = t.matmul(
                                            ps_y[:, bank * 512:(bank + 1) * 512],
                                            x2_fm[:, dc * SQ + tt * 128: dc * SQ + (tt + 1) * 128],
                                            M_sb[:, mb * 8 * E + dc * E + oc * 512: mb * 8 * E + dc * E + (oc + 1) * 512],
                                            start=(dc == 0), stop=(dc == 7))
                                    return last
                                p_my = mk.t(fmm, wait=[("mr8%d" % mb, gml),
                                                       ("dve", ybank_free[bank])])
                                if tt == 15 and oc == 1:
                                    M_free[mb] = p_my

                                def fye(v, tt=tt, oc=oc, e=e, bank=bank):
                                    return v.scalar_tensor_tensor(
                                        out=y_sb[:, tt * E + oc * 512: tt * E + (oc + 1) * 512],
                                        in0=ps_y[:, bank * 512:(bank + 1) * 512],
                                        scalar=gates_sb[:, tt * 8 + e: tt * 8 + e + 1],
                                        in1=y_sb[:, tt * E + oc * 512: tt * E + (oc + 1) * 512],
                                        op0=ALU.mult, op1=ALU.add)
                                ybank_free[bank] = mk.v(fye, wait=[("pe", p_my),
                                                                   ("pool", y_init)])
                    y_done = mk.cnt["dve"]

                    x8_free = [0, 0]
                    o8_free = [0, 0]
                    st_free8 = [0, 0]
                    for tt in range(16):
                        buf = tt % 2

                        def fx(sy, tt=tt, buf=buf):
                            return sy.dma_start(out=x8[:, buf * E:(buf + 1) * E],
                                                in_=xnew_sc[tt * 128:(tt + 1) * 128, :])
                        h_x = mk.hd(fx, wait=[("dve", x8_free[buf]),
                                              ("xw0", mk.cnt["xw0"]), ("xw1", mk.cnt["xw1"])],
                                    sem="xr8%d" % buf)

                        sc8 = buf * 32
                        def fbn(v, tt=tt, sc8=sc8):
                            v.bn_stats(st8[:, sc8:sc8 + 6], y_sb[:, tt * E: tt * E + 512])
                            return v.bn_stats(st8[:, sc8 + 6:sc8 + 12],
                                              y_sb[:, tt * E + 512: tt * E + 1024])
                        d_b0 = mk.v(fbn, wait=[("dve", y_done), ("act", st_free8[buf])])

                        def fag(v, sc8=sc8):
                            return v.bn_aggr(st8[:, sc8 + 12:sc8 + 14], st8[:, sc8:sc8 + 12])
                        d_ag = mk.v(fag, wait=[("dve", d_b0)])

                        def fln(s, sc8=sc8):
                            return s.activation(out=st8[:, sc8 + 15:sc8 + 16],
                                                in_=st8[:, sc8 + 13:sc8 + 14],
                                                func=AF.Ln, bias=eps_col[:], scale=1.0)
                        a_l = mk.s(fln, wait=[("dve", d_ag)])

                        def fex8(s, sc8=sc8):
                            return s.activation(out=st8[:, sc8 + 14:sc8 + 15],
                                                in_=st8[:, sc8 + 15:sc8 + 16],
                                                func=AF.Exp, bias=0.0, scale=-0.5)
                        a_e = mk.s(fex8, wait=[("act", a_l)])
                        st_free8[buf] = a_e

                        def fnm(v, tt=tt, buf=buf, sc8=sc8):
                            return v.tensor_scalar(
                                out=o8[:, buf * E:(buf + 1) * E],
                                in0=y_sb[:, tt * E:(tt + 1) * E],
                                scalar1=st8[:, sc8 + 12:sc8 + 13],
                                scalar2=st8[:, sc8 + 14:sc8 + 15],
                                op0=ALU.subtract, op1=ALU.mult)
                        d_nm = mk.v(fnm, wait=[("act", a_e), ("so%d" % buf, o8_free[buf])])

                        def fad(v, buf=buf):
                            return v.tensor_tensor(
                                out=o8[:, buf * E:(buf + 1) * E],
                                in0=o8[:, buf * E:(buf + 1) * E],
                                in1=x8[:, buf * E:(buf + 1) * E], op=ALU.add)
                        d_ad = mk.v(fad, wait=[("dve", d_nm), ("xr8%d" % buf, h_x)])
                        x8_free[buf] = d_ad

                        def fwy(sy, tt=tt, buf=buf):
                            return sy.dma_start(out=y_out[tt * 128:(tt + 1) * 128, :],
                                                in_=o8[:, buf * E:(buf + 1) * E])
                        o8_free[buf] = mk.hd(fwy, wait=[("dve", d_ad)], sem="so%d" % buf)

        with mk.phase("P_final"):
            def f(sy):
                return None
            snap = [(s, mk.cnt[s]) for s in mk.cnt if s not in ("sout", "gout")]
            mk._rec("sync", f, wait=snap + [("sout", mk.cnt["sout"])])
            def f2(g):
                return None
            mk._rec("gpsimd", f2, wait=[("gout", mk.cnt["gout"])])

    return nc, mk


from concourse.bass_utils import run_bass_kernel_spmd


def kernel(**inputs):
    f32 = lambda a: np.ascontiguousarray(np.asarray(a, np.float32))
    x = f32(inputs["L0_attn_hidden_states"])
    shared = {
        "Wq": f32(inputs["L0_attn_Wq"]), "Wk": f32(inputs["L0_attn_Wk"]),
        "Wv": f32(inputs["L0_attn_Wv"]), "Wo": f32(inputs["L0_attn_Wo"]),
        "bq": f32(inputs["L0_attn_bq"]), "bk": f32(inputs["L0_attn_bk"]),
        "bv": f32(inputs["L0_attn_bv"]), "bo": f32(inputs["L0_attn_bo"]),
        "relk": f32(inputs["L0_attn_rel_k"]), "relv": f32(inputs["L0_attn_rel_v"]),
        "wgate": f32(inputs["L0_ffn_w_gate"]), "win": f32(inputs["L0_ffn_w_in"]),
        "wout": f32(inputs["L0_ffn_w_out"]),
    }
    nc, _mk = build(upto="full", debug=())
    in_maps = []
    for core in range(8):
        b, half = core // 2, core % 2
        xb = x[b]
        x_local = np.concatenate([xb[SQ:], xb[:SQ]], axis=0) if half else xb
        m = np.full((128, 1), float(half), np.float32)
        in_maps.append(dict(shared, x_local=np.ascontiguousarray(x_local),
                            m_upper=m))
    res = run_bass_kernel_spmd(nc, in_maps, core_ids=list(range(8)))
    out = np.zeros((4, 4096, 1024), np.float32)
    for core in range(8):
        b, half = core // 2, core % 2
        out[b, half * SQ:(half + 1) * SQ] = res.results[core]["y"]
    return out


# version 04 (final)
# speedup vs baseline: 1.0022x; 1.0022x over previous
import numpy as np
from contextlib import ExitStack

import concourse.bass as bass
import concourse.mybir as mybir

F32 = mybir.dt.float32
BF16 = mybir.dt.bfloat16
AF = mybir.ActivationFunctionType
ALU = mybir.AluOpType
AX = mybir.AxisListType

S, SQ, E, H, HD = 4096, 2048, 1024, 16, 64
NEXP, HID = 8, 4096
EPS = 1e-5
SCL = 0.125
SEMS = ("pe", "dve", "act", "pool", "sout", "gout",
        "gs", "gb", "gw", "sx", "ks0", "ks1", "ks2", "ks3", "nk",
        "kp0", "kp1", "vp0", "vp1", "tt0", "tt1", "tcr0", "tcr1",
        "bb0", "bb1", "bb2", "bb3", "bdg0", "bdg1", "bdg2", "bdg3", "cbw", "cbr",
        "xo0", "xo1", "xw0", "xw1", "xr60", "xr61", "wo70", "wo71",
        "wi70", "wi71", "wi72", "wi73",
        "ms70", "ms71", "ms72", "ms73", "ms74", "ms75", "ms76", "ms77",
        "mr80", "mr81",
        "xr80", "xr81", "go5", "so0", "so1")
ENGS = ("sync", "gpsimd", "tensor", "vector", "scalar")


class MK:
    def __init__(self, nc, upto="full", debug=(), mini=False, cut=None):
        self.nc = nc
        self.upto = upto
        self.mini = mini
        self.cut = cut
        self.debug = set(debug)
        self.cnt = {s: 0 for s in SEMS}
        self.semh = {}
        self.cur = None
        self.last_wait = None
        self.dbg_names = []

    def _rec(self, engine, fn, wait=(), inc=None, inc_by=1):
        wl = []
        for (s, val) in wait:
            if val is None or val <= 0:
                continue
            if self.last_wait[engine].get(s, -1) >= val:
                continue
            self.last_wait[engine][s] = val
            wl.append((s, val))
        self.cur[engine].append((wl, fn, inc, inc_by))
        if inc is not None:
            self.cnt[inc] += inc_by
            return self.cnt[inc]
        return None

    def t(self, fn, wait=(), inc="pe"):
        return self._rec("tensor", fn, wait, inc)

    def v(self, fn, wait=(), inc="dve"):
        return self._rec("vector", fn, wait, inc)

    def s(self, fn, wait=(), inc="act"):
        return self._rec("scalar", fn, wait, inc)

    def g(self, fn, wait=(), inc="pool"):
        return self._rec("gpsimd", fn, wait, inc)

    def gd(self, fn, wait=(), sem="gs"):
        return self._rec("gpsimd", fn, wait, sem, 16)

    def hd(self, fn, wait=(), sem="sx"):
        return self._rec("sync", fn, wait, sem, 16)

    def so(self, fn, wait=()):
        return self._rec("sync", fn, wait, "sout", 16)

    def barrier(self):
        snap = [(s, self.cnt[s]) for s in self.cnt]
        for e in ENGS:
            self._rec(e, lambda eng: None, wait=snap)

    class _Phase:
        def __init__(self, mk, name):
            self.mk, self.name = mk, name

        def __enter__(self):
            self.mk.cur = {e: [] for e in ENGS}
            self.mk.last_wait = {e: {} for e in ENGS}
            self.mk.barrier()
            return self

        def __exit__(self, et, ev, tb):
            if et is not None:
                return False
            mk = self.mk
            with mk.nc.Block(self.name) as block:
                for e in ENGS:
                    ops = mk.cur[e]
                    if not ops:
                        continue

                    def body(eng, ops=ops):
                        for wl, fn, inc, inc_by in ops:
                            for (s, val) in wl:
                                eng.wait_ge(mk.semh[s], val)
                            ins = fn(eng)
                            if inc is not None:
                                assert ins is not None
                                ins.then_inc(mk.semh[inc], inc_by)
                    getattr(block, e)(body)
            mk.cur = None
            return False

    PH_ORDER = ["P0_consts", "P1_weights", "P2_ln1_qkv", "P3_norms",
                "P4_attn", "P5_oproj", "P6_ln2_gates", "P7_mbuild",
                "P8_apply", "P_final"]

    def enabled(self, name):
        if self.upto == "full":
            return True
        lim = {"p0": 0, "p1": 1, "p2": 2, "p3": 3, "p4": 4, "p5": 5,
               "p6": 6, "p7": 7, "p8": 8}.get(self.upto, 99)
        i = self.PH_ORDER.index(name) if name in self.PH_ORDER else 99
        return i <= lim or name == "P_final"

    class _NullPhase:
        def __enter__(self):
            return self

        def __exit__(self, et, ev, tb):
            return False

    def phase(self, name):
        if not self.enabled(name):
            return MK._Skip(self, name)
        return MK._Phase(self, name)

    class _Skip:
        def __init__(self, mk, name):
            self.mk = mk

        def __enter__(self):
            self.mk.cur = {e: [] for e in ENGS}
            self.mk.last_wait = {e: {} for e in ENGS}
            self.saved_cnt = dict(self.mk.cnt)
            return self

        def __exit__(self, et, ev, tb):
            if et is not None:
                return False
            self.mk.cnt = self.saved_cnt
            self.mk.cur = None
            return False

    def dbg(self, name, src_ap):
        if name not in self.debug or self.cur is None:
            return
        shape = list(src_ap.shape)
        h = self.nc.dram_tensor("dbg_" + name, shape, F32,
                                kind="ExternalOutput").ap()
        self.dbg_names.append("dbg_" + name)
        snap = [(s, self.cnt[s]) for s in self.cnt if s not in ("sout", "gout")]

        def f(g, h=h, src=src_ap):
            return g.dma_start(out=h, in_=src)
        self._rec("gpsimd", f, wait=snap, inc="gout", inc_by=16)


def build(upto="full", debug=(), mini=False, cut=None):
    nc = bass.Bass("TRN2", target_bir_lowering=False)
    mk = MK(nc, upto, debug, mini, cut)

    x_in = nc.dram_tensor("x_local", [S, E], F32, kind="ExternalInput").ap()
    Wd = {n: nc.dram_tensor(n, [E, E], F32, kind="ExternalInput").ap()
          for n in ("Wq", "Wk", "Wv", "Wo")}
    bd = {n: nc.dram_tensor(n, [E], F32, kind="ExternalInput").ap()
          for n in ("bq", "bk", "bv", "bo")}
    relk_d = nc.dram_tensor("relk", [41, HD], F32, kind="ExternalInput").ap()
    relv_d = nc.dram_tensor("relv", [41, HD], F32, kind="ExternalInput").ap()
    wgate_d = nc.dram_tensor("wgate", [NEXP, E], F32, kind="ExternalInput").ap()
    win_d = nc.dram_tensor("win", [NEXP, HID, E], F32, kind="ExternalInput").ap()
    wout_d = nc.dram_tensor("wout", [NEXP, E, HID], F32, kind="ExternalInput").ap()
    m_in = nc.dram_tensor("m_upper", [128, 1], F32, kind="ExternalInput").ap()
    y_out = nc.dram_tensor("y", [SQ, E], F32, kind="ExternalOutput").ap()

    k_bf = nc.dram_tensor("k_bf", [E, S], BF16).ap()
    v_bf = nc.dram_tensor("v_bf", [S, E], BF16).ap()
    TW = 295
    ttab = nc.dram_tensor("ttab", [4 * 128 * TW], F32).ap()
    dgt = nc.dram_tensor("dgt", [4 * (128 * 168 + 64)], BF16).ap()
    ctab = nc.dram_tensor("ctab", [2 * (128 * 21 + 32)], BF16).ap()
    xnew_sc = nc.dram_tensor("xnew_sc", [SQ, E], F32).ap()
    m_sc = nc.dram_tensor("m_sc", [NEXP, E, E], BF16).ap()

    outer = ExitStack()
    with outer:
        for s in SEMS:
            mk.semh[s] = outer.enter_context(nc.semaphore(s))

        def sb(st, name, p, f, dt):
            return st.enter_context(nc.sbuf_tensor(name, [p, f], dt))

        def psum(st, name, p, f, dt):
            return st.enter_context(nc.psum_tensor(name, [p, f], dt))

        ident = sb(outer, "ident", 128, 128, BF16)
        ident_f = sb(outer, "ident_f", 128, 128, F32)
        ones_bf = sb(outer, "ones_bf", 1, 128, BF16)
        ones_f32 = sb(outer, "ones_f32", 1, 128, F32)
        m_sb = sb(outer, "m_sb", 128, 1, F32)
        mn_sb = sb(outer, "mn_sb", 128, 1, F32)
        mneg_sb = sb(outer, "mneg_sb", 128, 1, F32)
        zgap = sb(outer, "zgap", 128, 32, BF16)
        eps_col = sb(outer, "eps_col", 128, 1, F32)
        mask_tl = sb(outer, "mask_tl", 128, 168, BF16)
        mask_tr = sb(outer, "mask_tr", 128, 168, BF16)
        mask_tl0 = sb(outer, "mask_tl0", 128, 148, BF16)
        mask_tr0 = sb(outer, "mask_tr0", 128, 148, BF16)
        mask_148 = sb(outer, "mask_148", 128, 41, BF16)
        mask_20 = sb(outer, "mask_20", 128, 20, BF16)
        mask_trb = sb(outer, "mask_trb", 128, 20, BF16)
        relk_nat = sb(outer, "relk_nat", 41, 64, BF16)
        relkT = sb(outer, "relkT", 128, 41, BF16)
        relv_sb = sb(outer, "relv_sb", 41, 64, BF16)
        wg_nat = sb(outer, "wg_nat", 8, E, BF16)
        wgT = sb(outer, "wgT", 128, 64, BF16)
        bq_sb = sb(outer, "bq_sb", 128, 8, F32)
        bk_sb = sb(outer, "bk_sb", 128, 8, F32)
        bo_row = sb(outer, "bo_row", 1, E, F32)
        bv_row = sb(outer, "bv_row", 1, E, BF16)
        mt_sb = sb(outer, "mt_sb", 128, 256, F32)
        gates_sb = sb(outer, "gates_sb", 128, 16 * 8, F32)
        sel16 = sb(outer, "sel16", 128, 8 * 16, BF16)



        with mk.phase("P0_consts"):
            def f(g):
                g.memset(ident[:], 1.0)
                g.memset(ident_f[:], 1.0)
                g.memset(ones_bf[:], 1.0)
                g.memset(ones_f32[:], 1.0)
                g.memset(zgap[:], 0.0)
                g.memset(eps_col[:], EPS)
                g.memset(sel16[:], 0.0)
                g.memset(mask_tl[:], 1.0)
                g.memset(mask_tr[:], 1.0)
                g.memset(mask_tl0[:], 1.0)
                g.memset(mask_tr0[:], 1.0)
                g.memset(mask_148[:], 1.0)
                g.memset(mask_20[:], 1.0)
                return g.memset(mask_trb[:], 1.0)
            c0 = mk.g(f)

            def f(g):
                g.affine_select(out=ident[:], in_=ident[:], pattern=[[-1, 128]],
                                compare_op=ALU.is_equal, fill=0.0, base=0,
                                channel_multiplier=1)
                g.affine_select(out=ident_f[:], in_=ident_f[:], pattern=[[-1, 128]],
                                compare_op=ALU.is_equal, fill=0.0, base=0,
                                channel_multiplier=1)
                g.affine_select(out=mask_tl[:], in_=mask_tl[:], pattern=[[-1, 168]],
                                compare_op=ALU.is_gt, fill=0.0, base=0,
                                channel_multiplier=1)
                g.affine_select(out=mask_tr[:], in_=mask_tr[:], pattern=[[1, 168]],
                                compare_op=ALU.is_ge, fill=0.0, base=-41,
                                channel_multiplier=-1)
                g.affine_select(out=mask_tl0[:], in_=mask_tl0[:], pattern=[[-1, 148]],
                                compare_op=ALU.is_gt, fill=0.0, base=-21,
                                channel_multiplier=1)
                g.affine_select(out=mask_tr0[:], in_=mask_tr0[:], pattern=[[1, 148]],
                                compare_op=ALU.is_ge, fill=0.0, base=-21,
                                channel_multiplier=-1)
                g.affine_select(out=mask_148[:], in_=mask_148[:], pattern=[[1, 41]],
                                compare_op=ALU.is_ge, fill=0.0, base=-148,
                                channel_multiplier=1)
                g.affine_select(out=mask_20[:], in_=mask_20[:], pattern=[[-1, 20]],
                                compare_op=ALU.is_ge, fill=0.0, base=19,
                                channel_multiplier=-1)
                return g.affine_select(out=mask_trb[:], in_=mask_trb[:],
                                       pattern=[[1, 20]], compare_op=ALU.is_ge,
                                       fill=0.0, base=107, channel_multiplier=-1)
            c1 = mk.g(f, wait=[("pool", c0)])

            def f(g):
                last = None
                for c in range(8):
                    g.memset(sel16[0:64, c * 16 + 2 * c: c * 16 + 2 * c + 1], 1.0)
                    last = g.memset(sel16[64:128, c * 16 + 2 * c + 1: c * 16 + 2 * c + 2], 1.0)
                return last
            c2 = mk.g(f, wait=[("pool", c1)])

            for srcdst in ("m", "relk", "relv", "wg", "bv"):
                def f(g, srcdst=srcdst):
                    if srcdst == "m":
                        return g.dma_start(out=m_sb[:], in_=m_in)
                    if srcdst == "relk":
                        return g.dma_start(out=relk_nat[:], in_=relk_d)
                    if srcdst == "relv":
                        return g.dma_start(out=relv_sb[:], in_=relv_d)
                    if srcdst == "wg":
                        return g.dma_start(out=wg_nat[:], in_=wgate_d)
                    return g.dma_start(out=bv_row[:], in_=bd["bv"].rearrange("(o e) -> o e", o=1))
                mk.gd(f, wait=[("pool", c2)])
            g_small = mk.cnt["gs"]

            def f(v):
                v.tensor_scalar(out=mn_sb[:], in0=m_sb[:], scalar1=-1.0,
                                scalar2=1.0, op0=ALU.mult, op1=ALU.add)
                return v.tensor_scalar(out=mneg_sb[:], in0=m_sb[:], scalar1=-1.0,
                                       scalar2=None, op0=ALU.mult)
            mk.v(f, wait=[("gs", g_small)])

            def f(sy):
                return sy.dma_start(out=bo_row[:], in_=bd["bo"].rearrange("(o e) -> o e", o=1))
            mk.hd(f, sem="gb")

        sAB = ExitStack()
        with sAB:
            w_sb = sb(sAB, "w_sb", 128, 8 * SQ, BF16)
            ps_mf = psum(sAB, "ps_mf", 128, 512, F32)

            sA = ExitStack()
            with sA:
                q_fm = sb(sA, "q_fm", 128, 8 * SQ, BF16)

                sQW = ExitStack()
                WqT = sb(sQW, "WqT", 128, 8 * E, BF16)
                WkT = sb(sQW, "WkT", 128, 8 * E, BF16)
                WvT = sb(sQW, "WvT", 128, 8 * E, BF16)
                sW = ExitStack()
                with sW:
                    wnat = sb(sW, "wnat", 128, 8 * E, BF16)
                    bq_row = sb(sW, "bq_row", 1, E, F32)
                    bk_row = sb(sW, "bk_row", 1, E, F32)
                    ps_wt = psum(sW, "ps_wt", 128, 2048, BF16)

                    with mk.phase("P1_weights"):
                        def f(sy):
                            return sy.dma_start(out=bq_row[:], in_=bd["bq"].rearrange("(o e) -> o e", o=1))
                        mk.hd(f, sem="gb")
                        def f(sy):
                            return sy.dma_start(out=bk_row[:], in_=bd["bk"].rearrange("(o e) -> o e", o=1))
                        mk.hd(f, sem="gb")
                        g_bias_rows = mk.cnt["gb"]
                        def f(t):
                            return t.transpose(ps_wt[0:64, 0:41], relk_nat[:],
                                               ident[0:41, 0:41])
                        pk = mk.t(f, wait=[("gs", g_small), ("pool", mk.cnt["pool"])])

                        def f(v):
                            v.tensor_copy(relkT[0:64, :], ps_wt[0:64, 0:41])
                            return v.tensor_copy(relkT[64:128, :], ps_wt[0:64, 0:41])
                        d_rk = mk.v(f, wait=[("pe", pk)])

                        def f(t):
                            last = None
                            for dc in range(8):
                                last = t.transpose(ps_wt[0:128, 1024 + dc * 8:1024 + dc * 8 + 8],
                                                   wg_nat[:, dc * 128:(dc + 1) * 128],
                                                   ident[0:8, 0:8])
                            return last
                        pw = mk.t(f, wait=[("dve", d_rk)])

                        def f(v):
                            return v.tensor_copy(wgT[:], ps_wt[:, 1024:1024 + 64])
                        d_wg = mk.v(f, wait=[("pe", pw)])

                        def f(t):
                            last = None
                            for oc in range(8):
                                last = t.transpose(ps_mf[:, oc:oc + 1],
                                                   bq_row[0:1, oc * 128:(oc + 1) * 128],
                                                   ident_f[0:1, 0:1])
                            for oc in range(8):
                                last = t.transpose(ps_mf[:, 8 + oc:8 + oc + 1],
                                                   bk_row[0:1, oc * 128:(oc + 1) * 128],
                                                   ident_f[0:1, 0:1])
                            return last
                        p_b = mk.t(f, wait=[("gb", g_bias_rows), ("dve", d_wg), ("pool", mk.cnt["pool"])])

                        def f(v):
                            v.tensor_copy(bq_sb[:], ps_mf[:, 0:8])
                            return v.tensor_copy(bk_sb[:], ps_mf[:, 8:16])
                        g_bias_d = mk.v(f, wait=[("pe", p_b)])

                        prev_tail = [0, 0]
                        for wname, dst in (("Wq", WqT), ("Wk", WkT), ("Wv", WvT)):
                            def f(g, wname=wname):
                                src = bass.AP(Wd[wname].tensor, 0,
                                              [[E, 128], [128 * E, 8], [1, E]])
                                return g.dma_start(out=wnat[:], in_=src)
                            gl = mk.gd(f, wait=[("pe", mk.cnt["pe"])], sem="gw")
                            ev_marks = []
                            for c in range(8):
                                def ftr(t, c=c):
                                    last = None
                                    for dc in range(8):
                                        o = (c % 2) * 1024 + dc * 128
                                        last = t.transpose(
                                            ps_wt[:, o:o + 128],
                                            wnat[:, c * E + dc * 128:c * E + dc * 128 + 128],
                                            ident[:])
                                    return last
                                waits = [("gw", gl), ("dve", g_bias_d)]
                                waits.append(("dve", ev_marks[c - 2] if c >= 2 else prev_tail[c]))
                                pc = mk.t(ftr, wait=waits)

                                def fev(v, c=c, dst=dst):
                                    dest = bass.AP(dst, c * 128,
                                                   [[8 * E, 128], [E, 8], [1, 128]])
                                    return v.tensor_copy(dest,
                                                         ps_wt[:, (c % 2) * 1024:(c % 2) * 1024 + 1024])
                                ev_marks.append(mk.v(fev, wait=[("pe", pc)]))
                            prev_tail = ev_marks[6:8]
                            mk.dbg("WT_" + wname, dst[:])

                    sW.close()
                    sX = ExitStack()
                    with sX:
                        xt = sb(sX, "xt", 128, 4 * E, F32)
                        x1c = sb(sX, "x1c", 128, 2 * 4 * E, BF16)
                        x1f = sb(sX, "x1f", 128, 2 * 8 * 512, BF16)
                        stq = sb(sX, "stq", 128, 128, F32)
                        kv_ev = sb(sX, "kv_ev", 128, 4 * 512, BF16)
                        ps_qkv = psum(sX, "ps_qkv", 128, 2 * 512, F32)
                        ps_v = psum(sX, "ps_v", 128, 2 * 512, F32)
                        ps_x1t = psum(sX, "ps_x1t", 128, 2 * 1024, BF16)

                        with mk.phase("P2_ln1_qkv"):
                            xb_free_all = 0
                            x1t_free = [0, 0]
                            qkv_bank_free = [0, 0]
                            v_bank_free = [0, 0]
                            kv_slot_free = [0, 0, 0, 0]
                            stat_free = [0] * 8
                            for tc in range(8):
                                buf = tc % 2
                                own = tc < 4

                                def f(sy, tc=tc):
                                    src = bass.AP(x_in.tensor, tc * 512 * E,
                                                  [[E, 128], [128 * E, 4], [1, E]])
                                    return sy.dma_start(out=xt[:], in_=src)
                                h_lx = mk.hd(f, wait=[("dve", xb_free_all)], sem="sx")

                                norm_marks = []
                                for tt in range(4):
                                    tix = xt[:, tt * E:(tt + 1) * E]
                                    tox = x1c[:, buf * 4 * E + tt * E: buf * 4 * E + (tt + 1) * E]
                                    g8 = buf * 4 + tt
                                    sc = g8 * 16
                                    def f(v, tix=tix, sc=sc):
                                        v.bn_stats(stq[:, sc:sc + 6], tix[:, 0:512])
                                        return v.bn_stats(stq[:, sc + 6:sc + 12],
                                                          tix[:, 512:1024])
                                    d_bn0 = mk.v(f, wait=[("sx", h_lx),
                                                          ("act", stat_free[g8])])

                                    def f(v, sc=sc):
                                        return v.bn_aggr(stq[:, sc + 12:sc + 14],
                                                         stq[:, sc:sc + 12])
                                    d_bn = mk.v(f, wait=[("dve", d_bn0)])

                                    def f(s, sc=sc):
                                        return s.activation(out=stq[:, sc + 15:sc + 16],
                                                            in_=stq[:, sc + 13:sc + 14],
                                                            func=AF.Ln, bias=eps_col[:],
                                                            scale=1.0)
                                    a1 = mk.s(f, wait=[("dve", d_bn)])

                                    def f(s, sc=sc):
                                        return s.activation(out=stq[:, sc + 14:sc + 15],
                                                            in_=stq[:, sc + 15:sc + 16],
                                                            func=AF.Exp, bias=0.0,
                                                            scale=-0.5)
                                    a2 = mk.s(f, wait=[("act", a1)])
                                    stat_free[g8] = a2

                                    def f(v, tix=tix, tox=tox, sc=sc):
                                        return v.tensor_scalar(
                                            out=tox, in0=tix,
                                            scalar1=stq[:, sc + 12:sc + 13],
                                            scalar2=stq[:, sc + 14:sc + 15],
                                            op0=ALU.subtract, op1=ALU.mult)
                                    d_norm = mk.v(f, wait=[("act", a2)])
                                    norm_marks.append(d_norm)

                                    def f(t, tox=tox, tt=tt):
                                        last = None
                                        for dc in range(8):
                                            o = (tt % 2) * 1024 + dc * 128
                                            last = t.transpose(
                                                ps_x1t[:, o:o + 128],
                                                tox[:, dc * 128:(dc + 1) * 128], ident[:])
                                        return last
                                    p_tx = mk.t(f, wait=[("dve", max(d_norm, x1t_free[tt % 2]))])

                                    def f(v, tt=tt, buf=buf):
                                        dest = bass.AP(x1f, buf * 8 * 512 + tt * 128,
                                                       [[2 * 8 * 512, 128], [512, 8], [1, 128]])
                                        return v.tensor_copy(
                                            dest, ps_x1t[:, (tt % 2) * 1024:(tt % 2) * 1024 + 1024])
                                    x1t_free[tt % 2] = mk.v(f, wait=[("pe", p_tx)])
                                xb_free_all = max(norm_marks)
                                x1_ready = mk.cnt["dve"]

                                kinds = [("k", WkT, bk_sb)] + ([("q", WqT, bq_sb)] if own else [])
                                for kind, WT, bias_cols in kinds:
                                    for oc in range(8):
                                        bank = oc % 2

                                        def fmm(t, WT=WT, oc=oc, bank=bank, buf=buf):
                                            last = None
                                            for dc in range(8):
                                                last = t.matmul(
                                                    ps_qkv[:, bank * 512:(bank + 1) * 512],
                                                    WT[:, dc * E + oc * 128: dc * E + oc * 128 + 128],
                                                    x1f[:, buf * 8 * 512 + dc * 512: buf * 8 * 512 + (dc + 1) * 512],
                                                    start=(dc == 0), stop=(dc == 7))
                                            return last
                                        pmm = mk.t(fmm, wait=[("dve", x1_ready),
                                                              ("act", qkv_bank_free[bank]),
                                                              ("dve", g_bias_d)])
                                        if kind == "q":
                                            def fev(s, oc=oc, bank=bank, tc=tc, bias_cols=bias_cols):
                                                return s.activation(
                                                    out=q_fm[:, oc * SQ + tc * 512: oc * SQ + (tc + 1) * 512],
                                                    in_=ps_qkv[:, bank * 512:(bank + 1) * 512],
                                                    func=AF.Identity,
                                                    bias=bias_cols[:, oc:oc + 1], scale=1.0)
                                            qkv_bank_free[bank] = mk.s(fev, wait=[("pe", pmm)])
                                        else:
                                            slot = oc % 4

                                            def fev(s, bank=bank, slot=slot, oc=oc, bias_cols=bias_cols):
                                                return s.activation(
                                                    out=kv_ev[:, slot * 512:(slot + 1) * 512],
                                                    in_=ps_qkv[:, bank * 512:(bank + 1) * 512],
                                                    func=AF.Identity,
                                                    bias=bias_cols[:, oc:oc + 1], scale=1.0)
                                            aev = mk.s(fev, wait=[("pe", pmm),
                                                                  ("ks%d" % slot, kv_slot_free[slot])])
                                            qkv_bank_free[bank] = aev

                                            def fdm(sy, oc=oc, slot=slot, tc=tc):
                                                return sy.dma_start(
                                                    out=k_bf[oc * 128:(oc + 1) * 128,
                                                             tc * 512:(tc + 1) * 512],
                                                    in_=kv_ev[:, slot * 512:(slot + 1) * 512])
                                            kv_slot_free[slot] = mk.hd(fdm, wait=[("act", aev)], sem="ks%d" % slot)

                                for tt in range(4):
                                    for oh in range(2):
                                        bank = (tt * 2 + oh) % 2

                                        def fmv(t, tt=tt, oh=oh, bank=bank, buf=buf):
                                            for dc in range(8):
                                                t.matmul(
                                                    ps_v[:, bank * 512:(bank + 1) * 512],
                                                    x1f[:, buf * 8 * 512 + dc * 512 + tt * 128: buf * 8 * 512 + dc * 512 + (tt + 1) * 128],
                                                    WvT[:, dc * E + oh * 512: dc * E + (oh + 1) * 512],
                                                    start=(dc == 0), stop=False)
                                            return t.matmul(
                                                ps_v[:, bank * 512:(bank + 1) * 512],
                                                ones_bf[0:1, 0:128],
                                                bv_row[0:1, oh * 512:(oh + 1) * 512],
                                                start=False, stop=True)
                                        pmv = mk.t(fmv, wait=[("dve", max(x1_ready, v_bank_free[bank])),
                                                              ("gs", g_small)])
                                        slot = (tt * 2 + oh) % 4

                                        def fev(v, bank=bank, slot=slot):
                                            return v.tensor_copy(
                                                kv_ev[:, slot * 512:(slot + 1) * 512],
                                                ps_v[:, bank * 512:(bank + 1) * 512])
                                        dev = mk.v(fev, wait=[("pe", pmv),
                                                              ("ks%d" % slot, kv_slot_free[slot])])
                                        v_bank_free[bank] = dev

                                        def fdm(sy, tc=tc, tt=tt, oh=oh, slot=slot):
                                            return sy.dma_start(
                                                out=v_bf[tc * 512 + tt * 128: tc * 512 + (tt + 1) * 128,
                                                         oh * 512:(oh + 1) * 512],
                                                in_=kv_ev[:, slot * 512:(slot + 1) * 512])
                                        kv_slot_free[slot] = mk.hd(fdm, wait=[("dve", dev)], sem="ks%d" % slot)

                            mk.dbg("q_fm", q_fm[:])
                            mk.dbg("x1f", x1f[:])

                sQW.close()
                sN = ExitStack()
                with sN:
                    nq2_sb = sb(sN, "nq2_sb", 128, SQ, F32)
                    nk2_sb = sb(sN, "nk2_sb", 128, S, F32)
                    sqk = sb(sN, "sqk", 128, S, BF16)
                    kchunk = sb(sN, "kchunk", 128, S, BF16)
                    kbv = sb(sN, "kbv", 128, 16, F32)
                    kb_row = sb(sN, "kb_row", 1, 16, F32)
                    kb_sb = sb(sN, "kb_sb", 128, 16, F32)
                    mtq_raw = sb(sN, "mtq_raw", 128, 256, F32)
                    ps_n = psum(sN, "ps_n", 128, 2 * 512, F32)

                    with mk.phase("P3_norms"):
                        nk_add = 0
                        bankfree = [0, 0]
                        for oc in range(8):
                            def f(sy, oc=oc):
                                return sy.dma_start(out=kchunk[:],
                                                    in_=k_bf[oc * 128:(oc + 1) * 128, :])
                            h_k = mk.hd(f, wait=[("pool", mk.cnt["pool"])], sem="nk")

                            def f(g, oc=oc):
                                return g.tensor_tensor(out=sqk[:], in0=kchunk[:],
                                                       in1=kchunk[:], op=ALU.mult)
                            g_sq = mk.g(f, wait=[("nk", h_k), ("pe", mk.cnt["pe"])])
                            for tcol in range(8):
                                bank = tcol % 2

                                def f(t, oc=oc, tcol=tcol, bank=bank):
                                    return t.matmul(ps_n[0:16, bank * 512:(bank + 1) * 512],
                                                    sel16[:, oc * 16:(oc + 1) * 16],
                                                    sqk[:, tcol * 512:(tcol + 1) * 512],
                                                    start=True, stop=True)
                                pm = mk.t(f, wait=[("pool", g_sq), ("dve", bankfree[bank])])

                                if oc == 0:
                                    def f(v, tcol=tcol, bank=bank):
                                        return v.tensor_copy(
                                            nk2_sb[0:16, tcol * 512:(tcol + 1) * 512],
                                            ps_n[0:16, bank * 512:(bank + 1) * 512])
                                else:
                                    def f(v, tcol=tcol, bank=bank):
                                        return v.tensor_tensor(
                                            out=nk2_sb[0:16, tcol * 512:(tcol + 1) * 512],
                                            in0=nk2_sb[0:16, tcol * 512:(tcol + 1) * 512],
                                            in1=ps_n[0:16, bank * 512:(bank + 1) * 512],
                                            op=ALU.add)
                                bankfree[bank] = mk.v(f, wait=[("pe", pm)])
                        for oc in range(8):
                            def f(g, oc=oc):
                                return g.tensor_tensor(out=sqk[:, 0:SQ],
                                                       in0=q_fm[:, oc * SQ:(oc + 1) * SQ],
                                                       in1=q_fm[:, oc * SQ:(oc + 1) * SQ],
                                                       op=ALU.mult)
                            g_sq = mk.g(f, wait=[("dve", bankfree[0]), ("dve", bankfree[1]),
                                                 ("pe", mk.cnt["pe"])])
                            for tcol in range(4):
                                bank = tcol % 2

                                def f(t, oc=oc, tcol=tcol, bank=bank):
                                    return t.matmul(ps_n[0:16, bank * 512:(bank + 1) * 512],
                                                    sel16[:, oc * 16:(oc + 1) * 16],
                                                    sqk[:, tcol * 512:(tcol + 1) * 512],
                                                    start=True, stop=True)
                                pm = mk.t(f, wait=[("pool", g_sq), ("dve", bankfree[bank])])
                                if oc == 0:
                                    def f(v, tcol=tcol, bank=bank):
                                        return v.tensor_copy(
                                            nq2_sb[0:16, tcol * 512:(tcol + 1) * 512],
                                            ps_n[0:16, bank * 512:(bank + 1) * 512])
                                else:
                                    def f(v, tcol=tcol, bank=bank):
                                        return v.tensor_tensor(
                                            out=nq2_sb[0:16, tcol * 512:(tcol + 1) * 512],
                                            in0=nq2_sb[0:16, tcol * 512:(tcol + 1) * 512],
                                            in1=ps_n[0:16, bank * 512:(bank + 1) * 512],
                                            op=ALU.add)
                                bankfree[bank] = mk.v(f, wait=[("pe", pm)])

                        def f(v):
                            return v.tensor_reduce(out=kbv[0:16, 0:1], in_=nk2_sb[0:16, :],
                                                   axis=AX.X, op=ALU.max)
                        d_km = mk.v(f, wait=[("dve", bankfree[0]), ("dve", bankfree[1])])

                        def f(s):
                            return s.activation(out=kbv[0:16, 1:2], in_=kbv[0:16, 0:1],
                                                func=AF.Ln, bias=0.0, scale=1.0)
                        a1 = mk.s(f, wait=[("dve", d_km)])

                        def f(s):
                            return s.activation(out=kbv[0:16, 2:3], in_=kbv[0:16, 1:2],
                                                func=AF.Exp, bias=0.0, scale=0.5)
                        a2 = mk.s(f, wait=[("act", a1)])

                        def f(v):
                            return v.tensor_scalar(out=kbv[0:16, 3:4], in0=kbv[0:16, 2:3],
                                                   scalar1=0.125, scalar2=0.25,
                                                   op0=ALU.mult, op1=ALU.add)
                        d_kb = mk.v(f, wait=[("act", a2)])

                        def f(t):
                            return t.transpose(ps_mf[0:1, 0:16], kbv[0:16, 3:4],
                                               ident_f[0:16, 0:16])
                        p_tk = mk.t(f, wait=[("dve", d_kb)])

                        def f(v):
                            return v.tensor_copy(kb_row[:], ps_mf[0:1, 0:16])
                        d_kr = mk.v(f, wait=[("pe", p_tk)])

                        def f(t):
                            return t.matmul(ps_mf[:, 16:32], ones_f32[0:1, :],
                                            kb_row[0:1, :], start=True, stop=True)
                        p_kb = mk.t(f, wait=[("dve", d_kr)])

                        def f(v):
                            return v.tensor_copy(kb_sb[:], ps_mf[:, 16:32])
                        d_kbb = mk.v(f, wait=[("pe", p_kb)])

                        tr_marks = []
                        for qt in range(16):
                            def f(t, qt=qt):
                                return t.transpose(ps_mf[:, 32 + (qt % 8) * 16: 32 + (qt % 8) * 16 + 16],
                                                   nq2_sb[0:16, qt * 128:(qt + 1) * 128],
                                                   ident_f[0:16, 0:16])
                            w = [("dve", mk.cnt["dve"])]
                            p_tq = mk.t(f, wait=w)

                            def f(v, qt=qt):
                                return v.tensor_copy(mtq_raw[:, qt * 16:(qt + 1) * 16],
                                                     ps_mf[:, 32 + (qt % 8) * 16: 32 + (qt % 8) * 16 + 16])
                            tr_marks.append(mk.v(f, wait=[("pe", p_tq)]))
                        d_tr = mk.cnt["dve"]

                        def f(v):
                            return v.tensor_scalar(out=mtq_raw[:], in0=mtq_raw[:],
                                                   scalar1=1e-12, scalar2=None,
                                                   op0=ALU.add)
                        d_eps = mk.v(f, wait=[("dve", d_tr)])

                        def f(s):
                            return s.activation(out=mt_sb[:], in_=mtq_raw[:],
                                                func=AF.Ln, bias=0.0, scale=1.0)
                        a3 = mk.s(f, wait=[("dve", d_eps)])

                        def f(s):
                            return s.activation(out=mtq_raw[:], in_=mt_sb[:],
                                                func=AF.Exp, bias=0.0, scale=0.5)
                        a4 = mk.s(f, wait=[("act", a3)])

                        def f(v):
                            last = None
                            for qt in range(16):
                                last = v.tensor_tensor(
                                    out=mt_sb[:, qt * 16:(qt + 1) * 16],
                                    in0=mtq_raw[:, qt * 16:(qt + 1) * 16],
                                    in1=kb_sb[:], op=ALU.mult)
                            return last
                        d_mt = mk.v(f, wait=[("act", a4), ("dve", d_kbb)])

                        mk.dbg("mt", mt_sb[:])
                        mk.dbg("nk2", nk2_sb[0:16, :])
                        mk.dbg("nq2", nq2_sb[0:16, :])
                        mk.dbg("k_bfd", k_bf)
                        mk.dbg("v_bfd", v_bf)


                sT = ExitStack()
                with sT:
                    kp_sb = sb(sT, "kp_sb", 128, 2 * S, BF16)
                    vp_sb = sb(sT, "vp_sb", 128, 2 * S, BF16)
                    exp_sb = sb(sT, "exp_sb", 128, 4 * S, BF16)
                    expT = sb(sT, "expT", 128, 32 * 512, BF16)
                    C_sb = sb(sT, "C_sb", 128, 4 * 168, F32)
                    T_sb = sb(sT, "T_sb", 128, 2 * 295, F32)
                    P_sb = sb(sT, "P_sb", 128, 4 * 48, F32)
                    dg_sb = sb(sT, "dg_sb", 128, 4 * 48, BF16)
                    cd_sb = sb(sT, "cd_sb", 128, 24, BF16)
                    tri_sb = sb(sT, "tri_sb", 128, 168, BF16)
                    aev_sb = sb(sT, "aev_sb", 128, 4 * 48, BF16)
                    aet_sb = sb(sT, "aet_sb", 41, 512, BF16)
                    sc_sb = sb(sT, "sc_sb", 128, 4 * 16, F32)
                    zg_sb = sb(sT, "zg_sb", 128, 4, F32)
                    zrow_sb = sb(sT, "zrow_sb", 1, 512, F32)
                    zrec_sb = sb(sT, "zrec_sb", 1, 512, F32)
                    rzb_sb = sb(sT, "rzb_sb", 128, 512, F32)
                    blf_sb = sb(sT, "blf_sb", 128, 16, F32)
                    ps_s = psum(sT, "ps_s", 128, 2048, F32)
                    ps_t = psum(sT, "ps_t", 128, 2048, BF16)
                    ps_w = psum(sT, "ps_w", 128, 512, F32)

                    if mk.mini is True:
                        HPS, HHS, GS = [0], [0, 1], [0, 3]
                    elif mk.mini:
                        HPS, HHS, GS = mk.mini
                    else:
                        HPS, HHS, GS = list(range(8)), [0, 1], list(range(4))

                    with mk.phase("P4_attn"):
                        def f_winit(g_):
                            return g_.memset(w_sb[:], 0.0)
                        w_init = mk.g(f_winit)

                        def f_ctz(sy):
                            return sy.dma_start(
                                out=bass.AP(ctab.tensor, 0, [[1, 128 * 22]]),
                                in_=bass.AP(zgap, 0, [[32, 128], [1, 22]]))
                        mk.hd(f_ctz, sem="cbw")
                        kp_free = [0, 0]
                        vp_free = [0, 0]
                        expfree = [dict(pe=0, pool=0, bb=0, bdg=0, act=0, cbw=0)
                                   for _ in range(4)]
                        tband_free = [0, 0]
                        tsb_free = [0, 0]
                        c_free = [0] * 4
                        dg_free = [0, 0, 0, 0]
                        zc_free = [0] * 4
                        blf_free = [0] * 4
                        psb_free = 0
                        misc_free = 0
                        pst_free = [0, 0]
                        expT_free = 0
                        psw_free = 0
                        aet_free = 0
                        exp_c1_done = 0
                        bank_free_act = [0, 0, 0, 0]
                        pending_tr = None
                        tb = 0
                        zg_read_pe = 0
                        aet_read_pe = 0

                        def band_geom(qt):
                            bs = max(0, 128 * qt - 20)
                            be = 128 * qt + 148
                            return bs, min(be, S)

                        def emit_tr_batches(qtl, par, nbatches, start_b):
                            nonlocal tb
                            marks = expfree[par]
                            for b in range(start_b, start_b + nbatches):
                                bank = tb % 2
                                tb += 1

                                def ftr(t, qtl=qtl, par=par, b=b, bank=bank):
                                    last = None
                                    for j in range(8):
                                        jt = 8 * b + j
                                        last = t.transpose(
                                            ps_t[:, bank * 1024 + j * 128: bank * 1024 + (j + 1) * 128],
                                            exp_sb[:, par * S + jt * 128: par * S + (jt + 1) * 128],
                                            ident[:])
                                    return last
                                pc = mk.t(ftr, wait=[("act", exp_c1_done),
                                                     ("dve", pst_free[bank]),
                                                     ("pe", expT_free)])

                                def fev(v, qtl=qtl, b=b, bank=bank):
                                    dest = bass.AP(expT, qtl * 128 + (8 * b) * 512,
                                                   [[32 * 512, 128], [512, 8], [1, 128]])
                                    return v.tensor_copy(dest, ps_t[:, bank * 1024: bank * 1024 + 1024])
                                pst_free[bank] = mk.v(fev, wait=[("pe", pc)])
                                marks["pe"] = pc

                        for hp in HPS:
                            buf = hp % 2

                            def fkl(sy, hp=hp, buf=buf):
                                return sy.dma_start(out=kp_sb[:, buf * S:(buf + 1) * S],
                                                    in_=k_bf[hp * 128:(hp + 1) * 128, :])
                            kl = mk.hd(fkl, wait=[("pe", kp_free[buf])], sem="kp%d" % buf)

                            def fvl(sy, hp=hp, buf=buf):
                                src = bass.AP(v_bf.tensor, hp * 128,
                                              [[E, 128], [128 * E, 32], [1, 128]])
                                return sy.dma_start(out=vp_sb[:, buf * S:(buf + 1) * S], in_=src)
                            vl = mk.hd(fvl, wait=[("pe", vp_free[buf])], sem="vp%d" % buf)

                            for hh in HHS:
                                h = 2 * hp + hh
                                qb = 64 * hh
                                for g in GS:
                                    if mk.cut is not None and hh == 1 and mk.cut <= 1:
                                        continue
                                    ae_done_marks = []
                                    group_marks = []
                                    def fpm(t, hp=hp, hh=hh, g=g, qb=qb):
                                        last = None
                                        for qtl in range(4):
                                            qt = 4 * g + qtl
                                            qsl = q_fm[qb:qb + 64,
                                                       hp * SQ + qt * 128: hp * SQ + (qt + 1) * 128]
                                            last = t.matmul(ps_mf[:, qtl * 48: qtl * 48 + 41],
                                                            qsl, relkT[qb:qb + 64, :],
                                                            start=True, stop=True)
                                        return last
                                    ppm = mk.t(fpm, wait=[("dve", misc_free),
                                                          ("gs", mk.cnt["gs"])])

                                    def fpe(v):
                                        last = None
                                        for qtl in range(4):
                                            last = v.tensor_copy(
                                                P_sb[:, qtl * 48: qtl * 48 + 41],
                                                ps_mf[:, qtl * 48: qtl * 48 + 41])
                                        return last
                                    d_pe = mk.v(fpe, wait=[("pe", ppm), ("pool", psb_free)])
                                    misc_free = d_pe

                                    for qtl in range(4):
                                        qt = 4 * g + qtl
                                        par = qtl % 2
                                        po = qtl * 48
                                        mtcol = mt_sb[:, qt * 16 + h: qt * 16 + h + 1]

                                        def fb1(g_, po=po, qtl=qtl, mtcol=mtcol):
                                            return g_.tensor_scalar(
                                                out=blf_sb[:, qtl * 4: qtl * 4 + 1],
                                                in0=P_sb[:, po:po + 1], scalar1=SCL,
                                                scalar2=mtcol, op0=ALU.mult, op1=ALU.subtract)
                                        b1 = mk.v(fb1, wait=[("dve", d_pe), ("act", blf_free[qtl])])

                                        def fb2(g_, po=po, qtl=qtl, mtcol=mtcol):
                                            return g_.tensor_scalar(
                                                out=blf_sb[:, qtl * 4 + 2: qtl * 4 + 3],
                                                in0=P_sb[:, po + 40:po + 41], scalar1=SCL,
                                                scalar2=mtcol, op0=ALU.mult, op1=ALU.subtract)
                                        b2 = mk.v(fb2, wait=[("dve", b1)])

                                        def fb3(g_, qtl=qtl):
                                            return g_.tensor_scalar(
                                                out=blf_sb[:, qtl * 4 + 3: qtl * 4 + 4],
                                                in0=blf_sb[:, qtl * 4: qtl * 4 + 1],
                                                scalar1=m_sb[:, 0:1], scalar2=None,
                                                op0=ALU.mult)
                                        b3 = mk.v(fb3, wait=[("dve", b2)])

                                        def fb4(g_, qtl=qtl):
                                            return g_.scalar_tensor_tensor(
                                                out=blf_sb[:, qtl * 4 + 1: qtl * 4 + 2],
                                                in0=blf_sb[:, qtl * 4 + 2: qtl * 4 + 3],
                                                scalar=mn_sb[:, 0:1],
                                                in1=blf_sb[:, qtl * 4 + 3: qtl * 4 + 4],
                                                op0=ALU.mult, op1=ALU.add)
                                        b4 = mk.v(fb4, wait=[("dve", b3)])
                                        blf_ready = b4

                                        def ft1(g_, par=par):
                                            return g_.memset(T_sb[:, par * 295: par * 295 + 128], 0.0)
                                        t1 = mk.g(ft1, wait=[("tt%d" % par, tsb_free[par])])

                                        def ft2(g_, par=par, po=po):
                                            return g_.tensor_scalar(
                                                out=T_sb[:, par * 295 + 127: par * 295 + 168],
                                                in0=P_sb[:, po:po + 41],
                                                scalar1=P_sb[:, po:po + 1], scalar2=None,
                                                op0=ALU.subtract)
                                        t2 = mk.v(ft2, wait=[("pool", t1), ("dve", b4)])

                                        def ft3(g_, par=par, po=po):
                                            return g_.tensor_scalar(
                                                out=T_sb[:, par * 295 + 168: par * 295 + 295],
                                                in0=T_sb[:, par * 295: par * 295 + 127],
                                                scalar1=P_sb[:, po + 40:po + 41],
                                                scalar2=P_sb[:, po:po + 1],
                                                op0=ALU.add, op1=ALU.subtract)
                                        t3 = mk.v(ft3, wait=[("dve", t2)])

                                        def ftw(sy, par=par):
                                            dst = bass.AP(ttab.tensor, par * 128 * 295,
                                                          [[295, 128], [1, 295]])
                                            return sy.dma_start(out=dst, in_=T_sb[:, par * 295:(par + 1) * 295])
                                        tw = mk.hd(ftw, wait=[("dve", t3),
                                                              ("tcr%d" % par, tband_free[par])],
                                                   sem="tt%d" % par)
                                        tsb_free[par] = tw

                                        def ftc(sy, par=par, qtl=qtl):
                                            srcap = bass.AP(ttab.tensor, par * 128 * 295 + 127,
                                                            [[294, 128], [1, 168]])
                                            return sy.dma_start(out=C_sb[:, qtl * 168:(qtl + 1) * 168],
                                                                in_=srcap)
                                        tc = mk.hd(ftc, wait=[("tt%d" % par, tw),
                                                              ("dve", c_free[qtl])],
                                                   sem="tcr%d" % par)
                                        tband_free[par] = tc
                                        c_ready = tc

                                        group_marks.append((blf_ready, c_ready))
                                    psb_free = mk.cnt["pool"]

                                    if mk.cut is not None and hh == 1 and mk.cut <= 2:
                                        continue
                                    for qtl in range(4):
                                        qt = 4 * g + qtl
                                        par = qtl
                                        bs, be = band_geom(qt)
                                        blf_ready, c_ready = group_marks[qtl]
                                        zbase = qtl * 16
                                        qsl_ = q_fm[qb:qb + 64,
                                                    hp * SQ + qt * 128: hp * SQ + (qt + 1) * 128]
                                        bank_marks = [0] * 8
                                        stt_last = 0
                                        def kinds_for(c0, c1):
                                            out = []
                                            if qt <= 14:
                                                segs = [(0, be, 0), (be, 2048, 2), (2048, 4096, 1)]
                                            else:
                                                segs = [(0, 2068, 0), (2068, 4096, 1)]
                                            for (s0, s1, kd) in segs:
                                                a0, a1 = max(c0, s0), min(c1, s1)
                                                if a0 < a1:
                                                    out.append((a0, a1, kd))
                                            return out
                                        zsl = []
                                        acts_all = []
                                        band_done = 0
                                        b7_done = 0
                                        for b8 in range(8):
                                            pb = b8 % 4
                                            c0, c1 = b8 * 512, (b8 + 1) * 512

                                            def fsb(t, b8=b8, pb=pb, qsl_=qsl_, buf=buf, qb=qb):
                                                return t.matmul(
                                                    ps_s[:, pb * 512:(pb + 1) * 512], qsl_,
                                                    kp_sb[qb:qb + 64, buf * S + b8 * 512: buf * S + (b8 + 1) * 512],
                                                    start=True, stop=True)
                                            psb_m = mk.t(fsb, wait=[("act", bank_free_act[pb]),
                                                                    ("kp%d" % buf, kl)])
                                            kp_free[buf] = psb_m
                                            if b8 == 3 and pending_tr is not None:
                                                emit_tr_batches(*pending_tr, 2, 0)
                                            if b8 == 7 and pending_tr is not None:
                                                emit_tr_batches(*pending_tr, 2, 2)
                                                pending_tr = None
                                            sttm = None
                                            bb0, bb1 = max(c0, bs), min(c1, be)
                                            if qt == 15 and b8 == 4:
                                                bb0 = bb1
                                            if bb0 < bb1:
                                                coff = (20 if qt == 0 else 0) + (bb0 - bs)

                                                def fba(v, qtl=qtl, pb=pb, bb0=bb0, bb1=bb1, c0=c0, coff=coff):
                                                    return v.tensor_tensor(
                                                        out=ps_s[:, pb * 512 + bb0 - c0: pb * 512 + bb1 - c0],
                                                        in0=ps_s[:, pb * 512 + bb0 - c0: pb * 512 + bb1 - c0],
                                                        in1=C_sb[:, qtl * 168 + coff: qtl * 168 + coff + bb1 - bb0],
                                                        op=ALU.add)
                                                sttm = mk.v(fba, wait=[("pe", psb_m),
                                                                       ("tcr%d" % (qtl % 2), c_ready)])
                                            if qt == 0 and b8 == 3:
                                                def fc2a(v, qtl=qtl, pb=pb):
                                                    return v.scalar_tensor_tensor(
                                                        out=ps_s[:, pb * 512 + 492: pb * 512 + 512],
                                                        in0=C_sb[:, qtl * 168: qtl * 168 + 20],
                                                        scalar=m_sb[:, 0:1],
                                                        in1=ps_s[:, pb * 512 + 492: pb * 512 + 512],
                                                        op0=ALU.mult, op1=ALU.add)
                                                sttm = mk.v(fc2a, wait=[("pe", psb_m), ("dve", sttm or 0),
                                                                        ("tcr%d" % (qtl % 2), c_ready)])
                                            if qt == 0 and b8 == 7:
                                                def fc2b(v, qtl=qtl, pb=pb):
                                                    return v.scalar_tensor_tensor(
                                                        out=ps_s[:, pb * 512 + 492: pb * 512 + 512],
                                                        in0=C_sb[:, qtl * 168: qtl * 168 + 20],
                                                        scalar=m_sb[:, 0:1],
                                                        in1=ps_s[:, pb * 512 + 492: pb * 512 + 512],
                                                        op0=ALU.mult, op1=ALU.add)
                                                sttm = mk.v(fc2b, wait=[("pe", psb_m),
                                                                        ("tcr%d" % (qtl % 2), c_ready)])
                                            if qt == 15 and b8 == 4:
                                                def fc3a(v, qtl=qtl, pb=pb):
                                                    return v.scalar_tensor_tensor(
                                                        out=ps_s[:, pb * 512: pb * 512 + 20],
                                                        in0=C_sb[:, qtl * 168 + 148: qtl * 168 + 168],
                                                        scalar=mn_sb[:, 0:1],
                                                        in1=ps_s[:, pb * 512: pb * 512 + 20],
                                                        op0=ALU.mult, op1=ALU.add)
                                                sttm = mk.v(fc3a, wait=[("pe", psb_m),
                                                                        ("tcr%d" % (qtl % 2), c_ready)])
                                            if sttm is not None:
                                                stt_last = sttm
                                            lastm = 0
                                            for (a0, a1, kd) in kinds_for(c0, c1):
                                                zslot = len(zsl)
                                                zsl.append(kd)

                                                def fex(s, a0=a0, a1=a1, kd=kd, qtl=qtl, par=par, pb=pb, c0=c0, zslot=zslot, zbase=zbase):
                                                    bias = blf_sb[:, qtl * 4 + kd: qtl * 4 + kd + 1]
                                                    return s.activation(
                                                        out=exp_sb[:, par * S + a0: par * S + a1],
                                                        in_=ps_s[:, pb * 512 + a0 - c0: pb * 512 + a1 - c0],
                                                        func=AF.Exp, bias=bias, scale=SCL,
                                                        accum_out=sc_sb[:, zbase + zslot: zbase + zslot + 1])
                                                w_ = [("pe", psb_m), ("dve", max(stt_last, blf_ready)),
                                                      ("pe", expfree[par]["pe"]),
                                                      ("pool", max(expfree[par]["pool"],
                                                                   zc_free[qtl][0] if isinstance(zc_free[qtl], tuple) else zc_free[qtl])),
                                                      ("dve", zc_free[qtl][1] if isinstance(zc_free[qtl], tuple) else 0),
                                                      ("bb%d" % par, expfree[par]["bb"]),
                                                      ("bdg%d" % par, expfree[par]["bdg"]),
                                                      ("cbw", expfree[par]["cbw"])]
                                                lastm = mk.s(fex, wait=w_)
                                                acts_all.append(lastm)
                                            bank_marks[b8] = lastm
                                            bank_free_act[pb] = lastm
                                            if c0 < be <= c1 or (bs < c1 and be > c1):
                                                band_done = max(band_done, lastm)
                                        exp_last = bank_marks[7]
                                        bband_hi = min(((be - 1) // 512), 7)
                                        band_src_done = bank_marks[bband_hi]
                                        exp_c1_done = exp_last
                                        acts0 = acts_all
                                        acts1 = [exp_last]
                                        c_free[qtl] = stt_last if stt_last else c_free[qtl]
                                        expfree[par]["act"] = exp_last

                                        def fbw(sy, par=par, qt=qt, bs=bs, be=be):
                                            base = par * (128 * 168 + 64)
                                            if qt == 0:
                                                return sy.dma_start(
                                                    out=bass.AP(dgt.tensor, base + 20, [[168, 128], [1, 148]]),
                                                    in_=exp_sb[:, par * S: par * S + 148])
                                            return sy.dma_start(
                                                out=bass.AP(dgt.tensor, base, [[168, 128], [1, 168]]),
                                                in_=exp_sb[:, par * S + bs: par * S + be])
                                        bw = mk.hd(fbw, wait=[("act", band_src_done),
                                                              ("bdg%d" % par, expfree[par]["bdg"])],
                                                   sem="bb%d" % par)
                                        expfree[par]["bb"] = bw
                                        if qt == 0:
                                            def fbz(sy, par=par):
                                                base = par * (128 * 168 + 64)
                                                return sy.dma_start(
                                                    out=bass.AP(dgt.tensor, base, [[168, 128], [1, 20]]),
                                                    in_=zgap[:, 0:20])
                                            bw = mk.hd(fbz, wait=[], sem="bb%d" % par)
                                            expfree[par]["bb"] = bw

                                        def fdg(sy, par=par):
                                            base = par * (128 * 168 + 64)
                                            return sy.dma_start(
                                                out=dg_sb[:, par * 48: par * 48 + 41],
                                                in_=bass.AP(dgt.tensor, base, [[169, 128], [1, 41]]))
                                        dgr = mk.hd(fdg, wait=[("bb%d" % par, bw),
                                                               ("pool", dg_free[par][0] if isinstance(dg_free[par], tuple) else dg_free[par]),
                                                               ("dve", dg_free[par][1] if isinstance(dg_free[par], tuple) else 0)],
                                                    sem="bdg%d" % par)
                                        expfree[par]["bdg"] = dgr
                                        if qt == 0:
                                            def fcw(sy, par=par):
                                                return sy.dma_start(
                                                    out=bass.AP(ctab.tensor, 0, [[21, 128], [1, 20]]),
                                                    in_=exp_sb[:, par * S + 4076: par * S + 4096])
                                            cw = mk.hd(fcw, wait=[("act", exp_c1_done),
                                                                  ("cbr", mk.cnt["cbr"]),
                                                                  ("cbw", mk.cnt["cbw"])],
                                                       sem="cbw")
                                            expfree[par]["cbw"] = cw

                                            def fcr(sy):
                                                return sy.dma_start(
                                                    out=cd_sb[:, 0:20],
                                                    in_=bass.AP(ctab.tensor, 0, [[22, 128], [1, 20]]))
                                            cr = mk.hd(fcr, wait=[("cbw", cw), ("pool", mk.cnt["pool"])],
                                                       sem="cbr")
                                        chain = {"pool": mk.cnt["pool"], "dve": 0}

                                        def pstep(fn, extra=()):
                                            m = mk.g(fn, wait=[("pool", chain["pool"]),
                                                               ("dve", chain["dve"])] + list(extra))
                                            chain["pool"] = m
                                            return m

                                        def vstep(fn, extra=()):
                                            m = mk.v(fn, wait=[("pool", chain["pool"]),
                                                               ("dve", chain["dve"])] + list(extra))
                                            chain["dve"] = m
                                            return m

                                        if qt == 0:
                                            tl_mask, tr_mask, tw_ = mask_tl0, mask_tr0, 148
                                            tslice = exp_sb[:, par * S: par * S + 148]
                                        else:
                                            tl_mask, tr_mask, tw_ = mask_tl, mask_tr, 168
                                            tslice = exp_sb[:, par * S + bs: par * S + be]

                                        pstep(lambda g_, tslice=tslice, tl_mask=tl_mask, tw_=tw_:
                                              g_.tensor_tensor(out=tri_sb[:, 0:tw_], in0=tslice,
                                                               in1=tl_mask[:], op=ALU.mult),
                                              extra=[("act", band_src_done)])
                                        vstep(lambda v, zbase=zbase, tw_=tw_:
                                              v.tensor_reduce(out=sc_sb[:, zbase + 10: zbase + 11],
                                                              in_=tri_sb[:, 0:tw_], axis=AX.X,
                                                              op=ALU.add))
                                        pstep(lambda g_, tslice=tslice, tr_mask=tr_mask, tw_=tw_:
                                              g_.tensor_tensor(out=tri_sb[:, 0:tw_], in0=tslice,
                                                               in1=tr_mask[:], op=ALU.mult))
                                        vstep(lambda v, zbase=zbase, tw_=tw_:
                                              v.tensor_reduce(out=sc_sb[:, zbase + 11: zbase + 12],
                                                              in_=tri_sb[:, 0:tw_], axis=AX.X,
                                                              op=ALU.add))
                                        if qt == 15:
                                            pstep(lambda g_, par=par, bs=bs:
                                                  g_.tensor_tensor(out=tri_sb[:, 0:20],
                                                                   in0=exp_sb[:, par * S + bs + 148: par * S + bs + 168],
                                                                   in1=mask_trb[:], op=ALU.mult),
                                                  extra=[("act", exp_c1_done)])
                                            vstep(lambda v, zbase=zbase:
                                                  v.tensor_reduce(out=sc_sb[:, zbase + 12: zbase + 13],
                                                                  in_=tri_sb[:, 0:20], axis=AX.X,
                                                                  op=ALU.add))
                                        vstep(lambda v, par=par, zbase=zbase:
                                              v.tensor_reduce(out=sc_sb[:, zbase + 13: zbase + 14],
                                                              in_=dg_sb[:, par * 48: par * 48 + 41],
                                                              axis=AX.X, op=ALU.add),
                                              extra=[("bdg%d" % par, dgr)])
                                        Lsl = [i for i, b in enumerate(zsl) if b == 0]
                                        Fsl = [i for i, b in enumerate(zsl) if b == 1]
                                        Rsl = [i for i, b in enumerate(zsl) if b == 2]
                                        zmk = max(acts0[-1], acts1[-1])
                                        po = qtl * 48
                                        for grp in (Lsl, Fsl, Rsl):
                                            for extra_i in grp[1:]:
                                                pstep(lambda g_, zbase=zbase, b0=grp[0], e0=extra_i:
                                                      g_.tensor_tensor(
                                                          out=sc_sb[:, zbase + b0: zbase + b0 + 1],
                                                          in0=sc_sb[:, zbase + b0: zbase + b0 + 1],
                                                          in1=sc_sb[:, zbase + e0: zbase + e0 + 1],
                                                          op=ALU.add),
                                                      extra=[("act", zmk)])
                                        zlb, zf = Lsl[0], Fsl[0]
                                        zr = Rsl[0] if Rsl else None
                                        pstep(lambda g_, zbase=zbase, qtl=qtl, zlb=zlb, zf=zf:
                                              g_.tensor_tensor(out=zg_sb[:, qtl:qtl + 1],
                                                               in0=sc_sb[:, zbase + zlb: zbase + zlb + 1],
                                                               in1=sc_sb[:, zbase + zf: zbase + zf + 1],
                                                               op=ALU.add),
                                              extra=[("act", zmk), ("pe", zg_read_pe)])
                                        if zr is not None:
                                            pstep(lambda g_, zbase=zbase, qtl=qtl, zr=zr:
                                                  g_.tensor_tensor(out=zg_sb[:, qtl:qtl + 1],
                                                                   in0=zg_sb[:, qtl:qtl + 1],
                                                                   in1=sc_sb[:, zbase + zr: zbase + zr + 1],
                                                                   op=ALU.add))
                                        if qt == 15:
                                            pstep(lambda g_, par=par:
                                                  g_.tensor_tensor(out=tri_sb[:, 0:41],
                                                                   in0=dg_sb[:, par * 48: par * 48 + 41],
                                                                   in1=mask_148[:], op=ALU.mult))
                                            vstep(lambda v, par=par, po=po:
                                                  v.scalar_tensor_tensor(
                                                      out=aev_sb[:, po:po + 41],
                                                      in0=tri_sb[:, 0:41],
                                                      scalar=mneg_sb[:, 0:1],
                                                      in1=dg_sb[:, par * 48: par * 48 + 41],
                                                      op0=ALU.mult, op1=ALU.add),
                                                  extra=[("pe", aet_read_pe)])
                                            vstep(lambda v, zbase=zbase:
                                                  v.tensor_reduce(out=sc_sb[:, zbase + 14: zbase + 15],
                                                                  in_=tri_sb[:, 0:41], axis=AX.X,
                                                                  op=ALU.add))
                                        else:
                                            pstep(lambda g_, par=par, po=po:
                                                  g_.tensor_copy(out=aev_sb[:, po:po + 41],
                                                                 in_=dg_sb[:, par * 48: par * 48 + 41]),
                                                  extra=[("pe", aet_read_pe)])
                                        pstep(lambda g_, zbase=zbase, zf=zf:
                                              g_.tensor_copy(out=sc_sb[:, zbase + 9: zbase + 10],
                                                             in_=sc_sb[:, zbase + zf: zbase + zf + 1]))
                                        if qt == 15:
                                            for addsl in (14, 12):
                                                pstep(lambda g_, zbase=zbase, addsl=addsl:
                                                      g_.tensor_tensor(
                                                          out=sc_sb[:, zbase + 9: zbase + 10],
                                                          in0=sc_sb[:, zbase + 9: zbase + 10],
                                                          in1=sc_sb[:, zbase + addsl: zbase + addsl + 1],
                                                          op=ALU.add))
                                        if qt == 0:
                                            pstep(lambda g_:
                                                  g_.tensor_tensor(out=cd_sb[:, 0:20],
                                                                   in0=cd_sb[:, 0:20],
                                                                   in1=mask_20[:], op=ALU.mult),
                                                  extra=[("cbr", cr)])
                                            vstep(lambda v, zbase=zbase:
                                                  v.tensor_reduce(out=sc_sb[:, zbase + 12: zbase + 13],
                                                                  in_=cd_sb[:, 0:20], axis=AX.X,
                                                                  op=ALU.add))
                                            pstep(lambda g_, zbase=zbase:
                                                  g_.tensor_tensor(
                                                      out=sc_sb[:, zbase + 9: zbase + 10],
                                                      in0=sc_sb[:, zbase + 9: zbase + 10],
                                                      in1=sc_sb[:, zbase + 12: zbase + 13],
                                                      op=ALU.subtract))
                                        pstep(lambda g_, zbase=zbase, zlb=zlb:
                                              g_.tensor_tensor(out=sc_sb[:, zbase + zlb: zbase + zlb + 1],
                                                               in0=sc_sb[:, zbase + zlb: zbase + zlb + 1],
                                                               in1=sc_sb[:, zbase + 11: zbase + 12],
                                                               op=ALU.subtract))
                                        pstep(lambda g_, zbase=zbase, zlb=zlb:
                                              g_.tensor_tensor(out=sc_sb[:, zbase + zlb: zbase + zlb + 1],
                                                               in0=sc_sb[:, zbase + zlb: zbase + zlb + 1],
                                                               in1=sc_sb[:, zbase + 13: zbase + 14],
                                                               op=ALU.subtract))
                                        pstep(lambda g_, zbase=zbase, zlb=zlb, par=par:
                                              g_.tensor_tensor(out=sc_sb[:, zbase + zlb: zbase + zlb + 1],
                                                               in0=sc_sb[:, zbase + zlb: zbase + zlb + 1],
                                                               in1=dg_sb[:, par * 48: par * 48 + 1],
                                                               op=ALU.add))
                                        vstep(lambda v, zbase=zbase, zlb=zlb, po=po:
                                              v.scalar_tensor_tensor(
                                                  out=aev_sb[:, po:po + 1],
                                                  in0=sc_sb[:, zbase + 9: zbase + 10],
                                                  scalar=m_sb[:, 0:1],
                                                  in1=sc_sb[:, zbase + zlb: zbase + zlb + 1],
                                                  op0=ALU.mult, op1=ALU.add))
                                        vstep(lambda v, zbase=zbase, zf=zf, po=po:
                                              v.scalar_tensor_tensor(
                                                  out=aev_sb[:, po + 40:po + 41],
                                                  in0=sc_sb[:, zbase + zf: zbase + zf + 1],
                                                  scalar=mn_sb[:, 0:1],
                                                  in1=sc_sb[:, zbase + 11: zbase + 12],
                                                  op0=ALU.mult, op1=ALU.add))
                                        if zr is not None:
                                            pstep(lambda g_, zbase=zbase, zr=zr, po=po:
                                                  g_.tensor_tensor(
                                                      out=aev_sb[:, po + 40:po + 41],
                                                      in0=aev_sb[:, po + 40:po + 41],
                                                      in1=sc_sb[:, zbase + zr: zbase + zr + 1],
                                                      op=ALU.add))
                                        if qt == 15:
                                            vstep(lambda v, zbase=zbase, po=po:
                                                  v.scalar_tensor_tensor(
                                                      out=aev_sb[:, po + 40:po + 41],
                                                      in0=sc_sb[:, zbase + 12: zbase + 13],
                                                      scalar=mneg_sb[:, 0:1],
                                                      in1=aev_sb[:, po + 40:po + 41],
                                                      op0=ALU.mult, op1=ALU.add))
                                        if qt == 0:
                                            vstep(lambda v, po=po:
                                                  v.scalar_tensor_tensor(
                                                      out=aev_sb[:, po:po + 20],
                                                      in0=cd_sb[:, 0:20],
                                                      scalar=m_sb[:, 0:1],
                                                      in1=aev_sb[:, po:po + 20],
                                                      op0=ALU.mult, op1=ALU.add))
                                        ae_tail = chain["pool"]
                                        ae_dve_tail = chain["dve"]
                                        dg_free[par] = (ae_tail, ae_dve_tail)
                                        zc_free[qtl] = (ae_tail, ae_dve_tail)
                                        blf_free[qtl] = exp_c1_done
                                        expfree[par]["pool"] = ae_tail
                                        ae_done_marks.append((ae_tail, ae_dve_tail))
                                        pending_tr = (qtl, par)

                                    if mk.cut is not None and hh == 1 and mk.cut <= 3:
                                        pending_tr = None
                                        continue
                                    emit_tr_batches(*pending_tr, 4, 0)
                                    pending_tr = None
                                    def fav1(t, hh=hh, buf=buf):
                                        last = None
                                        for jt in range(32):
                                            last = t.matmul(
                                                ps_w[0:64, 0:512],
                                                vp_sb[:, buf * S + jt * 128 + 64 * hh: buf * S + jt * 128 + 64 * hh + 64],
                                                expT[:, jt * 512:(jt + 1) * 512],
                                                start=(jt == 0), stop=False)
                                        return last
                                    pav1 = mk.t(fav1, wait=[("dve", max(pst_free[0], pst_free[1])),
                                                            ("vp%d" % buf, vl),
                                                            ("dve", psw_free)])
                                    vp_free[buf] = pav1
                                    bank = tb % 2
                                    tb += 1

                                    def fat(t, bank=bank):
                                        last = None
                                        for qtl in range(4):
                                            last = t.transpose(
                                                ps_t[0:41, bank * 1024 + qtl * 128: bank * 1024 + (qtl + 1) * 128],
                                                aev_sb[:, qtl * 48: qtl * 48 + 41],
                                                ident[:])
                                        return last
                                    pat = mk.t(fat, wait=[("pool", max(x[0] for x in ae_done_marks)),
                                                          ("dve", max(max(x[1] for x in ae_done_marks), pst_free[bank]))])

                                    def faev(v, bank=bank):
                                        return v.tensor_copy(aet_sb[:],
                                                             ps_t[0:41, bank * 1024: bank * 1024 + 512])
                                    d_aet = mk.v(faev, wait=[("pe", pat)])
                                    pst_free[bank] = d_aet
                                    def fzt_(t):
                                        last = None
                                        for qtl in range(4):
                                            last = t.transpose(ps_mf[0:1, qtl * 128:(qtl + 1) * 128],
                                                               zg_sb[:, qtl:qtl + 1], ident_f[:])
                                        return last
                                    pzt = mk.t(fzt_, wait=[("pool", max(x[0] for x in ae_done_marks)),
                                                           ("dve", max(max(x[1] for x in ae_done_marks), misc_free))])
                                    zg_read_pe = pzt

                                    def fze(v):
                                        return v.tensor_copy(zrow_sb[:], ps_mf[0:1, 0:512])
                                    d_ze = mk.v(fze, wait=[("pe", pzt)])

                                    def fzr(v):
                                        return v.reciprocal(zrec_sb[:], zrow_sb[:])
                                    d_zr = mk.v(fzr, wait=[("dve", d_ze)])

                                    def fzb(t):
                                        return t.matmul(ps_mf[0:64, 0:512], ones_f32[0:1, 0:64],
                                                        zrec_sb[0:1, :], start=True, stop=True)
                                    pzb = mk.t(fzb, wait=[("dve", d_zr)])

                                    def fre(v):
                                        return v.tensor_copy(rzb_sb[0:64, :], ps_mf[0:64, 0:512])
                                    d_re = mk.v(fre, wait=[("pe", pzb)])
                                    misc_free = d_re
                                    def fav2(t):
                                        last = None
                                        for qtl in range(4):
                                            last = t.matmul(
                                                ps_w[0:64, qtl * 128:(qtl + 1) * 128],
                                                relv_sb[0:41, 0:64],
                                                aet_sb[0:41, qtl * 128:(qtl + 1) * 128],
                                                start=False, stop=(qtl == 3))
                                        return last
                                    pav = mk.t(fav2, wait=[("dve", d_aet),
                                                           ("gs", mk.cnt["gs"])])
                                    expT_free = pav1
                                    aet_read_pe = pav

                                    def fwe(v, hh=hh, hp=hp, g=g):
                                        return v.tensor_tensor(
                                            out=w_sb[64 * hh:64 * hh + 64,
                                                     hp * SQ + g * 512: hp * SQ + (g + 1) * 512],
                                            in0=ps_w[0:64, :], in1=rzb_sb[0:64, :],
                                            op=ALU.mult)
                                    psw_free = mk.v(fwe, wait=[("pe", pav), ("dve", d_re),
                                                               ("pool", w_init)])

                        mk.dbg("w_sb", w_sb[:])
                        mk.dbg("aev0", aev_sb[:, 0:41])
                        mk.dbg("aev3", aev_sb[:, 144:185])
                        mk.dbg("zg", zg_sb[:])
                        mk.dbg("P0", P_sb[:, 0:41])
                        mk.dbg("sc", sc_sb[:, 0:8])
                        mk.dbg("rzb", rzb_sb[0:64, :])
                        mk.dbg("expd", exp_sb[:])
                        mk.dbg("blf", blf_sb[:])


            s5 = ExitStack()
            with s5:
                WoT = sb(s5, "WoT", 128, 8 * E, BF16)
                wnat5 = sb(s5, "wnat5", 128, 8 * E, BF16)
                ao_sb = sb(s5, "ao_sb", 128, 2 * E, F32)
                xo_sb = sb(s5, "xo_sb", 128, 2 * E, F32)
                on_sb = sb(s5, "on_sb", 128, 2 * E, F32)
                bo_bc = sb(s5, "bo_bc", 128, E, F32)
                st5 = sb(s5, "st5", 128, 64, F32)
                ps_o = psum(s5, "ps_o", 128, 1024, F32)
                ps_t5 = psum(s5, "ps_t5", 128, 2048, BF16)

                with mk.phase("P5_oproj"):
                    def f(g):
                        srcw = bass.AP(Wd["Wo"].tensor, 0,
                                       [[E, 128], [128 * E, 8], [1, E]])
                        return g.dma_start(out=wnat5[:], in_=srcw)
                    gl5 = mk.gd(f, sem="gw")
                    ev5 = []
                    for c in range(8):
                        def ftr(t, c=c):
                            last = None
                            for dc in range(8):
                                o = (c % 2) * 1024 + dc * 128
                                last = t.transpose(
                                    ps_t5[:, o:o + 128],
                                    wnat5[:, c * E + dc * 128:c * E + dc * 128 + 128],
                                    ident[:])
                            return last
                        waits = [("gw", gl5)]
                        if c >= 2:
                            waits.append(("dve", ev5[c - 2]))
                        pc = mk.t(ftr, wait=waits)

                        def fev(v, c=c):
                            dest = bass.AP(WoT, c * 128,
                                           [[8 * E, 128], [E, 8], [1, 128]])
                            return v.tensor_copy(dest,
                                                 ps_t5[:, (c % 2) * 1024:(c % 2) * 1024 + 1024])
                        ev5.append(mk.v(fev, wait=[("pe", pc)]))
                    wot_ready = ev5[-1]

                    def fbo(t):
                        t.matmul(ps_o[:, 0:512], ones_f32[0:1, :],
                                 bo_row[0:1, 0:512], start=True, stop=True)
                        return t.matmul(ps_o[:, 512:1024], ones_f32[0:1, :],
                                        bo_row[0:1, 512:1024], start=True, stop=True)
                    p_bo = mk.t(fbo, wait=[("gb", mk.cnt["gb"])])

                    def febo(v):
                        return v.tensor_copy(bo_bc[:], ps_o[:, 0:1024])
                    d_bo = mk.v(febo, wait=[("pe", p_bo)])

                    obank_free = [d_bo, d_bo]
                    xo_free = [0, 0]
                    on_free = [0, 0]
                    st_free = [0] * 2
                    for tt in range(16):
                        buf = tt % 2
                        def fx(sy, tt=tt, buf=buf):
                            srcx = bass.AP(x_in.tensor, tt * 128 * E,
                                           [[E, 128], [1, E]])
                            return sy.dma_start(out=xo_sb[:, buf * E:(buf + 1) * E],
                                                in_=srcx)
                        h_x = mk.hd(fx, wait=[("dve", xo_free[buf])], sem="xo%d" % buf)

                        for oc in range(2):
                            def fmo(t, tt=tt, oc=oc):
                                last = None
                                for dc in range(8):
                                    last = t.matmul(
                                        ps_o[:, oc * 512:(oc + 1) * 512],
                                        w_sb[:, dc * SQ + tt * 128: dc * SQ + (tt + 1) * 128],
                                        WoT[:, dc * E + oc * 512: dc * E + (oc + 1) * 512],
                                        start=(dc == 0), stop=(dc == 7))
                                return last
                            pmo = mk.t(fmo, wait=[("dve", max(wot_ready, obank_free[oc]))])

                            def feo(v, oc=oc, buf=buf):
                                return v.tensor_tensor(
                                    out=ao_sb[:, buf * E + oc * 512: buf * E + (oc + 1) * 512],
                                    in0=ps_o[:, oc * 512:(oc + 1) * 512],
                                    in1=bo_bc[:, oc * 512:(oc + 1) * 512], op=ALU.add)
                            obank_free[oc] = mk.v(feo, wait=[("pe", pmo)])
                        ao_done = obank_free[1]
                        sc5 = buf * 32
                        def fbn(v, buf=buf, sc5=sc5):
                            v.bn_stats(st5[:, sc5:sc5 + 6], ao_sb[:, buf * E: buf * E + 512])
                            return v.bn_stats(st5[:, sc5 + 6:sc5 + 12],
                                              ao_sb[:, buf * E + 512: buf * E + 1024])
                        d_b0 = mk.v(fbn, wait=[("dve", ao_done), ("act", st_free[buf])])

                        def fag(v, sc5=sc5):
                            return v.bn_aggr(st5[:, sc5 + 12:sc5 + 14], st5[:, sc5:sc5 + 12])
                        d_ag = mk.v(fag, wait=[("dve", d_b0)])

                        def fln(s, sc5=sc5):
                            return s.activation(out=st5[:, sc5 + 15:sc5 + 16],
                                                in_=st5[:, sc5 + 13:sc5 + 14],
                                                func=AF.Ln, bias=eps_col[:], scale=1.0)
                        a_l = mk.s(fln, wait=[("dve", d_ag)])

                        def fex5(s, sc5=sc5):
                            return s.activation(out=st5[:, sc5 + 14:sc5 + 15],
                                                in_=st5[:, sc5 + 15:sc5 + 16],
                                                func=AF.Exp, bias=0.0, scale=-0.5)
                        a_e = mk.s(fex5, wait=[("act", a_l)])
                        st_free[buf] = a_e

                        def fnm(v, buf=buf, sc5=sc5):
                            return v.tensor_scalar(
                                out=on_sb[:, buf * E:(buf + 1) * E],
                                in0=ao_sb[:, buf * E:(buf + 1) * E],
                                scalar1=st5[:, sc5 + 12:sc5 + 13],
                                scalar2=st5[:, sc5 + 14:sc5 + 15],
                                op0=ALU.subtract, op1=ALU.mult)
                        d_nm = mk.v(fnm, wait=[("act", a_e), ("xw%d" % buf, on_free[buf])])

                        def fad(v, buf=buf):
                            return v.tensor_tensor(
                                out=on_sb[:, buf * E:(buf + 1) * E],
                                in0=on_sb[:, buf * E:(buf + 1) * E],
                                in1=xo_sb[:, buf * E:(buf + 1) * E], op=ALU.add)
                        d_ad = mk.v(fad, wait=[("dve", d_nm), ("xo%d" % buf, h_x)])
                        xo_free[buf] = d_ad

                        def fwx(sy, tt=tt, buf=buf):
                            return sy.dma_start(out=xnew_sc[tt * 128:(tt + 1) * 128, :],
                                                in_=on_sb[:, buf * E:(buf + 1) * E])
                        on_free[buf] = mk.hd(fwx, wait=[("dve", d_ad)], sem="xw%d" % buf)
                    mk.dbg("xnew", xnew_sc)

        s_late = ExitStack()
        with s_late:
            x2_fm = sb(s_late, "x2_fm", 128, 8 * SQ, BF16)

            s6 = ExitStack()
            with s6:
                x6 = sb(s6, "x6", 128, 2 * E, F32)
                x2c = sb(s6, "x2c", 128, 2 * E, BF16)
                st6 = sb(s6, "st6", 128, 64, F32)
                gt6 = sb(s6, "gt6", 128, 64, F32)
                ps_l = psum(s6, "ps_l", 128, 512, F32)
                ps_t6 = psum(s6, "ps_t6", 128, 2048, BF16)

                with mk.phase("P6_ln2_gates"):
                    x6_free = [0, 0]
                    x2c_free = [0, 0]
                    st_free6 = [0, 0]
                    t6_free = [0, 0]
                    lg_free = 0
                    gt_free = [0, 0]
                    for tt in range(16):
                        buf = tt % 2

                        def fx(sy, tt=tt, buf=buf):
                            return sy.dma_start(out=x6[:, buf * E:(buf + 1) * E],
                                                in_=xnew_sc[tt * 128:(tt + 1) * 128, :])
                        h_x = mk.hd(fx, wait=[("dve", x6_free[buf]),
                                              ("xw0", mk.cnt["xw0"]), ("xw1", mk.cnt["xw1"])],
                                    sem="xr6%d" % buf)

                        sc6 = buf * 32
                        def fbn(v, buf=buf, sc6=sc6):
                            v.bn_stats(st6[:, sc6:sc6 + 6], x6[:, buf * E: buf * E + 512])
                            return v.bn_stats(st6[:, sc6 + 6:sc6 + 12],
                                              x6[:, buf * E + 512: buf * E + 1024])
                        d_b0 = mk.v(fbn, wait=[("xr6%d" % buf, h_x), ("act", st_free6[buf])])

                        def fag(v, sc6=sc6):
                            return v.bn_aggr(st6[:, sc6 + 12:sc6 + 14], st6[:, sc6:sc6 + 12])
                        d_ag = mk.v(fag, wait=[("dve", d_b0)])

                        def fln(s, sc6=sc6):
                            return s.activation(out=st6[:, sc6 + 15:sc6 + 16],
                                                in_=st6[:, sc6 + 13:sc6 + 14],
                                                func=AF.Ln, bias=eps_col[:], scale=1.0)
                        a_l = mk.s(fln, wait=[("dve", d_ag)])

                        def fex6(s, sc6=sc6):
                            return s.activation(out=st6[:, sc6 + 14:sc6 + 15],
                                                in_=st6[:, sc6 + 15:sc6 + 16],
                                                func=AF.Exp, bias=0.0, scale=-0.5)
                        a_e = mk.s(fex6, wait=[("act", a_l)])
                        st_free6[buf] = a_e

                        def fnm(v, buf=buf, sc6=sc6):
                            return v.tensor_scalar(
                                out=x2c[:, buf * E:(buf + 1) * E],
                                in0=x6[:, buf * E:(buf + 1) * E],
                                scalar1=st6[:, sc6 + 12:sc6 + 13],
                                scalar2=st6[:, sc6 + 14:sc6 + 15],
                                op0=ALU.subtract, op1=ALU.mult)
                        d_nm = mk.v(fnm, wait=[("act", a_e), ("pe", x2c_free[buf])])
                        x6_free[buf] = d_nm

                        def ftx(t, tt=tt, buf=buf):
                            last = None
                            for dc in range(8):
                                o = (tt % 2) * 1024 + dc * 128
                                last = t.transpose(
                                    ps_t6[:, o:o + 128],
                                    x2c[:, buf * E + dc * 128: buf * E + (dc + 1) * 128],
                                    ident[:])
                            return last
                        p_tx = mk.t(ftx, wait=[("dve", max(d_nm, t6_free[tt % 2]))])
                        x2c_free[buf] = p_tx

                        def fevx(v, tt=tt):
                            dest = bass.AP(x2_fm, tt * 128,
                                           [[8 * SQ, 128], [SQ, 8], [1, 128]])
                            return v.tensor_copy(dest,
                                                 ps_t6[:, (tt % 2) * 1024:(tt % 2) * 1024 + 1024])
                        d_evx = mk.v(fevx, wait=[("pe", p_tx)])
                        t6_free[tt % 2] = d_evx

                        def fml(t, tt=tt):
                            last = None
                            for dc in range(8):
                                last = t.matmul(
                                    ps_l[:, 0:8],
                                    x2_fm[:, dc * SQ + tt * 128: dc * SQ + (tt + 1) * 128],
                                    wgT[:, dc * 8:(dc + 1) * 8],
                                    start=(dc == 0), stop=(dc == 7))
                            return last
                        p_lg = mk.t(fml, wait=[("dve", max(d_evx, lg_free))])

                        gb = buf * 32
                        def fel(v, gb=gb):
                            return v.tensor_copy(gt6[:, gb:gb + 8], ps_l[:, 0:8])
                        d_lg = mk.v(fel, wait=[("pe", p_lg), ("act", gt_free[buf])])
                        lg_free = d_lg

                        gch = {"dve": d_lg, "act": 0}

                        def gv(fn, extra=()):
                            m_ = mk.v(fn, wait=[("dve", gch["dve"]), ("act", gch["act"])] + list(extra))
                            gch["dve"] = m_
                            return m_

                        def ga(fn, extra=()):
                            m_ = mk.s(fn, wait=[("dve", gch["dve"]), ("act", gch["act"])] + list(extra))
                            gch["act"] = m_
                            return m_

                        gv(lambda v, gb=gb: v.tensor_reduce(
                            out=gt6[:, gb + 8:gb + 9], in_=gt6[:, gb:gb + 8],
                            axis=AX.X, op=ALU.max))
                        gv(lambda v, gb=gb: v.tensor_scalar(
                            out=gt6[:, gb + 16:gb + 24], in0=gt6[:, gb:gb + 8],
                            scalar1=gt6[:, gb + 8:gb + 9], scalar2=None,
                            op0=ALU.is_ge))
                        gv(lambda v, gb=gb: v.scalar_tensor_tensor(
                            out=gt6[:, gb + 24:gb + 32], in0=gt6[:, gb + 16:gb + 24],
                            scalar=-1e30, in1=gt6[:, gb:gb + 8],
                            op0=ALU.mult, op1=ALU.add))
                        gv(lambda v, gb=gb: v.tensor_reduce(
                            out=gt6[:, gb + 9:gb + 10], in_=gt6[:, gb + 24:gb + 32],
                            axis=AX.X, op=ALU.max))
                        gv(lambda v, gb=gb: v.tensor_tensor(
                            out=gt6[:, gb + 10:gb + 11], in0=gt6[:, gb + 9:gb + 10],
                            in1=gt6[:, gb + 8:gb + 9], op=ALU.subtract))
                        ga(lambda s, gb=gb: s.activation(
                            out=gt6[:, gb + 11:gb + 12], in_=gt6[:, gb + 10:gb + 11],
                            func=AF.Exp, bias=0.0, scale=1.0))
                        gv(lambda v, gb=gb: v.tensor_scalar(
                            out=gt6[:, gb + 11:gb + 12], in0=gt6[:, gb + 11:gb + 12],
                            scalar1=1.0, scalar2=None, op0=ALU.add))
                        gv(lambda v, gb=gb: v.reciprocal(
                            gt6[:, gb + 12:gb + 13], gt6[:, gb + 11:gb + 12]))
                        gv(lambda v, gb=gb: v.tensor_scalar(
                            out=gt6[:, gb + 13:gb + 14], in0=gt6[:, gb + 12:gb + 13],
                            scalar1=-1.0, scalar2=1.0, op0=ALU.mult, op1=ALU.add))
                        gv(lambda v, gb=gb: v.tensor_scalar(
                            out=gt6[:, gb + 24:gb + 32], in0=gt6[:, gb + 24:gb + 32],
                            scalar1=gt6[:, gb + 9:gb + 10], scalar2=None,
                            op0=ALU.is_ge))
                        gv(lambda v, gb=gb: v.tensor_scalar(
                            out=gt6[:, gb + 16:gb + 24], in0=gt6[:, gb + 16:gb + 24],
                            scalar1=gt6[:, gb + 12:gb + 13], scalar2=None,
                            op0=ALU.mult))
                        glast = gv(lambda v, gb=gb, tt=tt: v.scalar_tensor_tensor(
                            out=gates_sb[:, tt * 8:(tt + 1) * 8],
                            in0=gt6[:, gb + 24:gb + 32],
                            scalar=gt6[:, gb + 13:gb + 14],
                            in1=gt6[:, gb + 16:gb + 24],
                            op0=ALU.mult, op1=ALU.add))
                        gt_free[buf] = gch["act"]
                    mk.dbg("gates", gates_sb[:])
                    mk.dbg("x2fm", x2_fm[:])

            s7 = ExitStack()
            with s7:
                woT = sb(s7, "woT", 128, 32 * E, BF16)
                woch = sb(s7, "woch", 128, 2 * HID, BF16)
                wsl = sb(s7, "wsl", 128, 2 * 4 * 512, BF16)
                m_ev = sb(s7, "m_ev", 128, 8 * 512, BF16)
                ps_m = psum(s7, "ps_m", 128, 4096, F32)
                psm_b = ps_m[:].bitcast(BF16)

                with mk.phase("P7_mbuild"):
                    woch_free = [0, 0]
                    wsl_free = [0, 0, 0, 0]
                    woT_free = 0
                    tr_free2 = [0, 0]
                    bank_free = [0] * 8
                    mev_free = [0] * 8
                    tr_free = 0
                    for e in range(NEXP):
                        tevs = []
                        for oc in range(8):
                            wb = oc % 2

                            def fwl(g, e=e, oc=oc, wb=wb):
                                return g.dma_start(
                                    out=woch[:, wb * HID:(wb + 1) * HID],
                                    in_=wout_d[e, oc * 128:(oc + 1) * 128, :])
                            gwo = mk.gd(fwl, wait=[("pe", woch_free[wb])], sem="wo7%d" % wb)

                            for jh in range(2):
                                def ftw(t, oc=oc, wb=wb, jh=jh):
                                    last = None
                                    for j2 in range(16):
                                        jc = jh * 16 + j2
                                        o = (jh % 2) * 2048 + j2 * 128
                                        last = t.transpose(
                                            psm_b[:, o:o + 128],
                                            woch[:, wb * HID + jc * 128: wb * HID + (jc + 1) * 128],
                                            ident[:])
                                    return last
                                waits = [("wo7%d" % wb, gwo), ("dve", tr_free2[jh % 2])]
                                if e > 0 or oc > 0:
                                    waits.append(("dve", max(bank_free)))
                                p_tw = mk.t(ftw, wait=waits)
                                if jh == 1:
                                    woch_free[wb] = p_tw

                                def fte(v, oc=oc, jh=jh):
                                    dest = bass.AP(woT, jh * 16 * E + oc * 128,
                                                   [[32 * E, 128], [E, 16], [1, 128]])
                                    return v.tensor_copy(
                                        dest, psm_b[:, (jh % 2) * 2048:(jh % 2) * 2048 + 2048])
                                tr_free2[jh % 2] = mk.v(fte, wait=[("pe", p_tw)])
                                tevs.append(tr_free2[jh % 2])
                        woT_ready = tevs[-1]

                        for dch in range(2):
                            for jq in range(8):
                                slot = jq % 2

                                def fwi(g, e=e, jq=jq, dch=dch, slot=slot):
                                    srcw = bass.AP(win_d.tensor,
                                                   e * HID * E + jq * 4 * 128 * E + dch * 512,
                                                   [[E, 128], [128 * E, 4], [1, 512]])
                                    return g.dma_start(
                                        out=wsl[:, slot * 2048:(slot + 1) * 2048], in_=srcw)
                                gwi = mk.gd(fwi, wait=[("pe", wsl_free[slot])],
                                            sem="wi7%d" % slot)

                                def fmm(t, jq=jq, dch=dch, slot=slot):
                                    last = None
                                    for j2 in range(4):
                                        jc = jq * 4 + j2
                                        for dq in range(4):
                                            for oh in range(2):
                                                bank = dq * 2 + oh
                                                last = t.matmul(
                                                    ps_m[:, bank * 512:(bank + 1) * 512],
                                                    wsl[:, slot * 2048 + j2 * 512 + dq * 128: slot * 2048 + j2 * 512 + (dq + 1) * 128],
                                                    woT[:, jc * E + oh * 512: jc * E + (oh + 1) * 512],
                                                    start=(jc == 0), stop=(jc == 31))
                                    return last
                                waits = [("wi7%d" % slot, gwi), ("dve", woT_ready)]
                                if jq == 0:
                                    waits.append(("dve", max(bank_free)))
                                    for bb_ in range(8):
                                        waits.append(("ms7%d" % bb_, mev_free[bb_]))
                                p_mm = mk.t(fmm, wait=waits)
                                wsl_free[slot] = p_mm
                            for dq in range(4):
                                for oh in range(2):
                                    bank = dq * 2 + oh

                                    def fme(v, bank=bank):
                                        return v.tensor_copy(
                                            m_ev[:, bank * 512:(bank + 1) * 512],
                                            ps_m[:, bank * 512:(bank + 1) * 512])
                                    d_me = mk.v(fme, wait=[("pe", p_mm), ("ms7%d" % bank, mev_free[bank])])
                                    bank_free[bank] = d_me

                                    def fmd(sy, e=e, dch=dch, dq=dq, oh=oh, bank=bank):
                                        dc = dch * 4 + dq
                                        return sy.dma_start(
                                            out=m_sc[e, dc * 128:(dc + 1) * 128,
                                                     oh * 512:(oh + 1) * 512],
                                            in_=m_ev[:, bank * 512:(bank + 1) * 512])
                                    mev_free[bank] = mk.hd(fmd, wait=[("dve", d_me)], sem="ms7%d" % bank)

            s8 = ExitStack()
            with s8:
                y_sb = sb(s8, "y_sb", 128, 16 * E, F32)
                M_sb = sb(s8, "M_sb", 128, 2 * 8 * E, BF16)
                x8 = sb(s8, "x8", 128, 2 * E, F32)
                o8 = sb(s8, "o8", 128, 2 * E, F32)
                st8 = sb(s8, "st8", 128, 64, F32)
                ps_y = psum(s8, "ps_y", 128, 4096, F32)

                with mk.phase("P8_apply"):
                    def fyz(g):
                        return g.memset(y_sb[:], 0.0)
                    y_init = mk.g(fyz)

                    ybank_free = [0] * 8
                    M_free = [0, 0]
                    y_dve = 0
                    for e in range(NEXP):
                        mb = e % 2

                        def fml(g, e=e, mb=mb):
                            srcm = bass.AP(m_sc.tensor, e * E * E,
                                           [[E, 128], [128 * E, 8], [1, E]])
                            return g.dma_start(out=M_sb[:, mb * 8 * E:(mb + 1) * 8 * E],
                                               in_=srcm)
                        gml = mk.gd(fml, wait=[("pe", M_free[mb])] +
                                    [("ms7%d" % bb_, mk.cnt["ms7%d" % bb_]) for bb_ in range(8)],
                                    sem="mr8%d" % mb)

                        for tt in range(16):
                            for oc in range(2):
                                bank = (tt * 2 + oc) % 8

                                def fmm(t, tt=tt, oc=oc, mb=mb, bank=bank):
                                    last = None
                                    for dc in range(8):
                                        last = t.matmul(
                                            ps_y[:, bank * 512:(bank + 1) * 512],
                                            x2_fm[:, dc * SQ + tt * 128: dc * SQ + (tt + 1) * 128],
                                            M_sb[:, mb * 8 * E + dc * E + oc * 512: mb * 8 * E + dc * E + (oc + 1) * 512],
                                            start=(dc == 0), stop=(dc == 7))
                                    return last
                                p_my = mk.t(fmm, wait=[("mr8%d" % mb, gml),
                                                       ("dve", ybank_free[bank])])
                                if tt == 15 and oc == 1:
                                    M_free[mb] = p_my

                                def fye(v, tt=tt, oc=oc, e=e, bank=bank):
                                    return v.scalar_tensor_tensor(
                                        out=y_sb[:, tt * E + oc * 512: tt * E + (oc + 1) * 512],
                                        in0=ps_y[:, bank * 512:(bank + 1) * 512],
                                        scalar=gates_sb[:, tt * 8 + e: tt * 8 + e + 1],
                                        in1=y_sb[:, tt * E + oc * 512: tt * E + (oc + 1) * 512],
                                        op0=ALU.mult, op1=ALU.add)
                                ybank_free[bank] = mk.v(fye, wait=[("pe", p_my),
                                                                   ("pool", y_init)])
                    y_done = mk.cnt["dve"]

                    x8_free = [0, 0]
                    o8_free = [0, 0]
                    st_free8 = [0, 0]
                    for tt in range(16):
                        buf = tt % 2

                        def fx(sy, tt=tt, buf=buf):
                            return sy.dma_start(out=x8[:, buf * E:(buf + 1) * E],
                                                in_=xnew_sc[tt * 128:(tt + 1) * 128, :])
                        h_x = mk.hd(fx, wait=[("dve", x8_free[buf]),
                                              ("xw0", mk.cnt["xw0"]), ("xw1", mk.cnt["xw1"])],
                                    sem="xr8%d" % buf)

                        sc8 = buf * 32
                        def fbn(v, tt=tt, sc8=sc8):
                            v.bn_stats(st8[:, sc8:sc8 + 6], y_sb[:, tt * E: tt * E + 512])
                            return v.bn_stats(st8[:, sc8 + 6:sc8 + 12],
                                              y_sb[:, tt * E + 512: tt * E + 1024])
                        d_b0 = mk.v(fbn, wait=[("dve", y_done), ("act", st_free8[buf])])

                        def fag(v, sc8=sc8):
                            return v.bn_aggr(st8[:, sc8 + 12:sc8 + 14], st8[:, sc8:sc8 + 12])
                        d_ag = mk.v(fag, wait=[("dve", d_b0)])

                        def fln(s, sc8=sc8):
                            return s.activation(out=st8[:, sc8 + 15:sc8 + 16],
                                                in_=st8[:, sc8 + 13:sc8 + 14],
                                                func=AF.Ln, bias=eps_col[:], scale=1.0)
                        a_l = mk.s(fln, wait=[("dve", d_ag)])

                        def fex8(s, sc8=sc8):
                            return s.activation(out=st8[:, sc8 + 14:sc8 + 15],
                                                in_=st8[:, sc8 + 15:sc8 + 16],
                                                func=AF.Exp, bias=0.0, scale=-0.5)
                        a_e = mk.s(fex8, wait=[("act", a_l)])
                        st_free8[buf] = a_e

                        def fnm(v, tt=tt, buf=buf, sc8=sc8):
                            return v.tensor_scalar(
                                out=o8[:, buf * E:(buf + 1) * E],
                                in0=y_sb[:, tt * E:(tt + 1) * E],
                                scalar1=st8[:, sc8 + 12:sc8 + 13],
                                scalar2=st8[:, sc8 + 14:sc8 + 15],
                                op0=ALU.subtract, op1=ALU.mult)
                        d_nm = mk.v(fnm, wait=[("act", a_e), ("so%d" % buf, o8_free[buf])])

                        def fad(v, buf=buf):
                            return v.tensor_tensor(
                                out=o8[:, buf * E:(buf + 1) * E],
                                in0=o8[:, buf * E:(buf + 1) * E],
                                in1=x8[:, buf * E:(buf + 1) * E], op=ALU.add)
                        d_ad = mk.v(fad, wait=[("dve", d_nm), ("xr8%d" % buf, h_x)])
                        x8_free[buf] = d_ad

                        def fwy(sy, tt=tt, buf=buf):
                            return sy.dma_start(out=y_out[tt * 128:(tt + 1) * 128, :],
                                                in_=o8[:, buf * E:(buf + 1) * E])
                        o8_free[buf] = mk.hd(fwy, wait=[("dve", d_ad)], sem="so%d" % buf)

        with mk.phase("P_final"):
            def f(sy):
                return None
            snap = [(s, mk.cnt[s]) for s in mk.cnt if s not in ("sout", "gout")]
            mk._rec("sync", f, wait=snap + [("sout", mk.cnt["sout"])])
            def f2(g):
                return None
            mk._rec("gpsimd", f2, wait=[("gout", mk.cnt["gout"])])

    return nc, mk


from concourse.bass_utils import run_bass_kernel_spmd


def kernel(**inputs):
    f32 = lambda a: np.ascontiguousarray(np.asarray(a, np.float32))
    x = f32(inputs["L0_attn_hidden_states"])
    shared = {
        "Wq": f32(inputs["L0_attn_Wq"]), "Wk": f32(inputs["L0_attn_Wk"]),
        "Wv": f32(inputs["L0_attn_Wv"]), "Wo": f32(inputs["L0_attn_Wo"]),
        "bq": f32(inputs["L0_attn_bq"]), "bk": f32(inputs["L0_attn_bk"]),
        "bv": f32(inputs["L0_attn_bv"]), "bo": f32(inputs["L0_attn_bo"]),
        "relk": f32(inputs["L0_attn_rel_k"]), "relv": f32(inputs["L0_attn_rel_v"]),
        "wgate": f32(inputs["L0_ffn_w_gate"]), "win": f32(inputs["L0_ffn_w_in"]),
        "wout": f32(inputs["L0_ffn_w_out"]),
    }
    nc, _mk = build(upto="full", debug=())
    in_maps = []
    for core in range(8):
        b, half = core // 2, core % 2
        xb = x[b]
        x_local = np.concatenate([xb[SQ:], xb[:SQ]], axis=0) if half else xb
        m = np.full((128, 1), float(half), np.float32)
        in_maps.append(dict(shared, x_local=np.ascontiguousarray(x_local),
                            m_upper=m))
    res = run_bass_kernel_spmd(nc, in_maps, core_ids=list(range(8)))
    out = np.zeros((4, 4096, 1024), np.float32)
    for core in range(8):
        b, half = core // 2, core % 2
        out[b, half * SQ:(half + 1) * SQ] = res.results[core]["y"]
    return out
